# version 20 (final)
# baseline (speedup 1.0000x reference)
import numpy as np
import ml_dtypes
import concourse.bass as bass
import concourse.mybir as mybir
from concourse.bass_utils import run_bass_kernel_spmd

F32 = mybir.dt.float32
BF16 = mybir.dt.bfloat16
I8 = mybir.dt.int8
AF = mybir.ActivationFunctionType
ALU = mybir.AluOpType
AX = mybir.AxisListType

D = 1024
N = 4096
E = 3072
IF = 2816
KC = 8
MCQ = 24
ICN = 22
HCN = 8
NT = 32
NTT = 8
QMAXI = 1.0 / 127.0


def build():
    nc = bass.Bass("TRN2", target_bir_lowering=False)

    x_h = nc.dram_tensor("x", [N, D], F32, kind="ExternalInput").ap()
    wq_h = nc.dram_tensor("w_qkva", [D, E], F32, kind="ExternalInput").ap()
    gam_h = nc.dram_tensor("gamma", [8, 128], F32, kind="ExternalInput").ap()
    wg_h = nc.dram_tensor("w_gate", [IF, D], F32, kind="ExternalInput").ap()
    wu_h = nc.dram_tensor("w_up", [IF, D], F32, kind="ExternalInput").ap()
    wd_h = nc.dram_tensor("w_down", [D, IF], F32, kind="ExternalInput").ap()
    clip_h = nc.dram_tensor("clips", [1, 2], F32, kind="ExternalInput").ap()
    idf_h = nc.dram_tensor("ident_f", [128, 128], F32, kind="ExternalInput").ap()
    out_h = nc.dram_tensor("out", [N, D], F32, kind="ExternalOutput").ap()
    xn_h = nc.dram_tensor("xn_sp", [N, D], F32).ap()
    qy_h = nc.dram_tensor("qy_sp", [D, N], F32).ap()
    v_h = nc.dram_tensor("v_sp", [IF, N], F32).ap()
    g8_h = nc.dram_tensor("g8_sp", [IF, D], I8).ap()
    u8_h = nc.dram_tensor("u8_sp", [IF, D], I8).ap()
    d8_h = nc.dram_tensor("d8_sp", [D, IF], I8).ap()
    cc1i = nc.dram_tensor("cc1i", [1, 8], F32)
    cc1o = nc.dram_tensor("cc1o", [1, 8], F32)
    cc2i = nc.dram_tensor("cc2i", [1, 8], F32)
    cc2o = nc.dram_tensor("cc2o", [1, 8], F32)

    A = nc.alloc_sbuf_tensor
    idf = A("idf", [128, 128], F32)
    gamt = A("gamt", [8, 128], F32)
    gamc = A("gamc", [128, 8], F32)
    clip_t = A("clip_t", [1, 2], F32)
    ones_r = A("ones_r", [1, 128], F32)
    bnst = A("bnst", [128, NT * 12], F32)
    aggr = A("aggr", [128, NT * 2], F32)
    muc = A("muc", [128, NT], F32)
    varc = A("varc", [128, NT], F32)
    rc = A("rc", [128, NT], F32)
    hscc = A("hscc", [128, NT], F32)
    rtmp = A("rtmp", [128, NT], F32)
    xmaxc = A("xmaxc", [128, NT], F32)
    vmaxc = A("vmaxc", [128, ICN], F32)
    vmaxrun = A("vmaxrun", [128, ICN], F32)
    vmaxr = A("vmaxr", [128, 1], F32)
    swg = A("swg", [128, ICN], F32)
    swu = A("swu", [128, ICN], F32)
    swd = A("swd", [128, HCN], F32)
    invg = A("invg", [128, ICN], F32)
    invu = A("invu", [128, ICN], F32)
    invd = A("invd", [128, HCN], F32)
    sgate = A("sgate", [128, ICN], F32)
    comb = A("comb", [128, ICN], F32)
    sdown = A("sdown", [128, HCN], F32)
    smt = A("smt", [1, 8], F32)
    smt2 = A("smt2", [1, 8], F32)
    inv1c = A("inv1c", [128, 1], F32)
    inv2c = A("inv2c", [128, 1], F32)
    s1col = A("s1col", [128, 1], F32)
    s2col = A("s2col", [128, 1], F32)
    s1s2 = A("s1s2", [1, 4], F32)
    carry = A("carry", [128, 8], F32)

    wbufA = A("wbufA", [128, 24576], BF16)
    wbufB = A("wbufB", [128, 24576], BF16)
    whi = wbufA[:].rearrange("p (kc mc f) -> p kc mc f", kc=KC, mc=MCQ)
    wlo = wbufB[:].rearrange("p (kc mc f) -> p kc mc f", kc=KC, mc=MCQ)
    qwg = wbufA[:, 0:KC * ICN * 128].rearrange(
        "p (kc ic f) -> p kc ic f", kc=KC, ic=ICN)
    qwu = wbufB[:, 0:KC * ICN * 128].rearrange(
        "p (kc ic f) -> p kc ic f", kc=KC, ic=ICN)
    qwd = wbufA[:, 0:ICN * HCN * 128].rearrange(
        "p (ic hc f) -> p ic hc f", ic=ICN, hc=HCN)
    h2b2 = [wbufB[:, 0:ICN * 512].rearrange("p (ic n) -> p ic n", ic=ICN),
            wbufB[:, ICN * 512:2 * ICN * 512].rearrange(
                "p (ic n) -> p ic n", ic=ICN)]

    wstg = A("wstg", [128, 2, 1024], F32)
    xqfm = wstg[:].rearrange("p a b -> p (a b)").bitcast(BF16).rearrange(
        "p (kc n) -> p kc n", kc=KC)
    w8stg = A("w8stg", [128, 2, 1024], I8)
    wdstg = A("wdstg", [128, 2, 2816], F32)
    vstage = wdstg[:].rearrange("p a b -> p (a b)").rearrange(
        "p (ic n) -> p ic n", ic=11)
    hilo2 = wdstg[:].rearrange("p a b -> p (a b)").bitcast(BF16)[
        :, 0:2 * KC * 512].rearrange(
        "p (a k n) -> p a k n", a=2, k=KC)

    xbuf = A("xbuf", [128, 2, 1024], F32)
    a1buf = A("a1buf", [128, 2, 1024], F32)
    xqfm2 = a1buf[:].rearrange("p a b -> p (a b)").bitcast(BF16).rearrange(
        "p (kc n) -> p kc n", kc=KC)
    xnb = A("xnb", [128, 2, 1024], F32)
    hqb = A("hqb", [128, 2, 1024], F32)
    outtm = hqb
    hilo = A("hilo", [128, 2, KC, 512], BF16)
    outsum = hilo[:].rearrange("p a k n -> p (a k n)").bitcast(F32).rearrange(
        "p (hc n) -> p hc n", hc=HCN)
    wbstg = hilo[:].rearrange("p a k n -> p (a k n)")[:, 0:4096].bitcast(
        F32).rearrange("p (a b) -> p a b", a=2)
    qf = A("qf", [128, 2, 512], F32)
    kvf = A("kvf", [128, 2, 512], F32)
    wd8b_s = [qf[:].rearrange("p a n -> p (a n)").bitcast(I8)[:, 0:2816],
              kvf[:].rearrange("p a n -> p (a n)").bitcast(I8)[:, 0:2816]]
    af = A("af", [128, 2, 512], F32)
    sigb = af
    yf = A("yf", [128, 2, 512], F32)
    vtmp = yf
    qyb = A("qyb", [128, 2, 512], F32)
    xq8 = A("xq8", [128, 2, 1024], I8)
    h28 = xq8

    pb = [nc.alloc_psum_tensor(f"pb{i}", [128, 512], F32) for i in range(8)]

    class Sem:
        def __init__(self, name):
            self.h = nc.alloc_semaphore(name=name)
            self.v = 0

        def bump(self, n=1):
            self.v += n
            return self.v

    CST = Sem("cst")
    WQ = [Sem("wq0"), Sem("wq1")]
    FN = [Sem("fn0"), Sem("fn1")]
    FW = [Sem("fw0"), Sem("fw1")]
    FQN = [Sem("fqn0"), Sem("fqn1")]
    FQW = [Sem("fqw0"), Sem("fqw1")]
    XL = [Sem("xl0"), Sem("xl1")]
    A1L = [Sem("a1l0"), Sem("a1l1")]
    XNS = [Sem("xns0"), Sem("xns1")]
    QYS = [Sem(f"qys{g}") for g in range(8)]
    VS = Sem("vs")
    OS = [Sem("os0"), Sem("os1")]
    GUL = [Sem("gul0"), Sem("gul1")]
    DL = [Sem("dl0"), Sem("dl1")]
    VL = [Sem("vl0"), Sem("vl1")]
    QYL = Sem("qyl")
    CC = Sem("cc")
    CCE = Sem("cce")
    PE = Sem("pe")
    DV = Sem("dv")
    AC = Sem("ac")
    GP = Sem("gp")

    ops = {"sp": [], "gp": [], "pe": [], "dv": [], "ac": []}

    def em(eng, waits, fn):
        waits = tuple(waits)
        def g(e, waits=waits, fn=fn):
            for (s, v) in waits:
                if v > 0:
                    e.wait_ge(s.h, v)
            fn(e)
        ops[eng].append(g)

    def dma(eng, sem, out_ap, in_ap, waits=()):
        val = sem.bump(16)
        def g(e, out_ap=out_ap, in_ap=in_ap, waits=tuple(waits)):
            for (s, v) in waits:
                if v > 0:
                    e.wait_ge(s.h, v)
            e.dma_start(out=out_ap, in_=in_ap).then_inc(sem.h, 16)
        ops[eng].append(g)
        return val

    dma("sp", CST, idf[:], idf_h)
    dma("sp", CST, gamt[:], gam_h)
    w_cst = dma("sp", CST, clip_t[:], clip_h)

    em("dv", [], lambda e: e.memset(ones_r[:], 1.0).then_inc(DV.h, 1))
    v_ones = DV.bump()

    em("pe", [(CST, w_cst)],
       lambda e: e.transpose(pb[6][0:128, 0:8], gamt[:, :], idf[0:8, 0:8]).then_inc(PE.h, 1))
    v_gt = PE.bump()
    em("dv", [(PE, v_gt)],
       lambda e: e.tensor_copy(gamc[:], pb[6][:, 0:8]).then_inc(DV.h, 1))
    v_gc = DV.bump()

    xld1 = {}
    bn_dv = {}
    v_row3b = {}

    def emit_rowmath(b):
        sl = slice(b * 8, b * 8 + 8)

        def f(e, sl=sl):
            e.tensor_copy(muc[:, sl],
                          aggr[:].rearrange("p (t k) -> p t k", k=2)[:, sl, 0])
            e.tensor_copy(varc[:, sl],
                          aggr[:].rearrange("p (t k) -> p t k", k=2)[:, sl, 1])
            e.drain()
            e.tensor_scalar_add(rtmp[:, sl], varc[:, sl], 1e-5)
            e.drain()
            e.reciprocal(rc[:, sl], rtmp[:, sl])
            e.drain().then_inc(DV.h, 1)
        em("dv", [(DV, bn_dv[b * 8 + 7])], f)
        v_row1 = DV.bump()

        em("ac", [(DV, v_row1)],
           lambda e, sl=sl: e.activation(rc[:, sl], rc[:, sl],
                                         AF.Sqrt).then_inc(AC.h, 1))
        v_rsq = AC.bump()

        def f(e, sl=sl):
            e.tensor_mul(rtmp[:, sl], rc[:, sl], rc[:, sl])
            e.drain()
            e.tensor_mul(rtmp[:, sl], rtmp[:, sl], varc[:, sl])
            e.drain()
            e.tensor_scalar_mul(rtmp[:, sl], rtmp[:, sl], 1024.0)
            e.drain().then_inc(DV.h, 1)
        em("dv", [(AC, v_rsq)], f)
        v_row2 = DV.bump()

        em("ac", [(DV, v_row2)],
           lambda e, sl=sl: e.activation(rtmp[:, sl], rtmp[:, sl],
                                         AF.Sqrt).then_inc(AC.h, 1))
        v_rsq2 = AC.bump()

        def f(e, sl=sl):
            e.tensor_scalar_max(rtmp[:, sl], rtmp[:, sl], 1e-12)
            e.drain()
            e.reciprocal(rtmp[:, sl], rtmp[:, sl])
            e.drain()
            e.tensor_scalar_mul(rtmp[:, sl], rtmp[:, sl], 32.0)
            e.drain()
            e.tensor_mul(hscc[:, sl], rc[:, sl], rtmp[:, sl])
            e.drain().then_inc(DV.h, 1)
        em("dv", [(AC, v_rsq2)], f)
        v_row3b[b] = DV.bump()

    def emit_a1(i):
        slot = i % 2
        xld1[i] = (A1L[slot], dma("gp", A1L[slot], a1buf[:, slot, :],
                                 x_h[i * 128:(i + 1) * 128, :],
                                 [(DV, bn_dv.get(i - 2, 0))]))

        def f(e, i=i, slot=slot):
            e.bn_stats(bnst[:, i * 12:i * 12 + 6], a1buf[:, slot, 0:512])
            e.bn_stats(bnst[:, i * 12 + 6:i * 12 + 12],
                       a1buf[:, slot, 512:1024])
            e.drain()
            e.bn_aggr(aggr[:, i * 2:i * 2 + 2],
                      bnst[:, i * 12:i * 12 + 12]).then_inc(DV.h, 1)
        em("dv", [xld1[i]], f)
        bn_dv[i] = DV.bump()
        if i < MCQ:
            emit_wq(i)
        elif i < MCQ + HCN:
            emit_down(i - MCQ)
        if i % 8 == 7:
            emit_rowmath(i // 8)

    fq_ac = {}
    fq_st = {}

    def emit_down(ch):
        slot = ch % 2
        LSEM = FW[slot]
        waits = []
        if ch >= 2:
            waits.append((AC, fq_ac[("d", ch - 2)]))
        ld_v = (LSEM, dma("ac", LSEM, wdstg[:, slot, :],
                          wd_h[ch * 128:(ch + 1) * 128, :], waits))

        def f(e, ld_v=ld_v, slot=slot, ch=ch):
            e.tensor_reduce(swd[:, ch:ch + 1], wdstg[:, slot, :], AX.X,
                            ALU.max, apply_absolute_value=True)
            e.drain()
            e.reciprocal(invd[:, ch:ch + 1], swd[:, ch:ch + 1])
            e.drain()
            e.tensor_scalar_mul(invd[:, ch:ch + 1], invd[:, ch:ch + 1],
                                127.0).then_inc(DV.h, 1)
        em("dv", [ld_v], f)
        v_sc = DV.bump()

        st_prev = fq_st[("d", ch - 2)] if ch >= 2 else (FQW[0], 0)

        def f(e, v_sc=v_sc, slot=slot, ch=ch):
            e.activation(wd8b_s[slot], wdstg[:, slot, :], AF.Copy,
                         scale=invd[:, ch:ch + 1]).then_inc(AC.h, 1)
        em("ac", [(DV, v_sc), st_prev], f)
        fq_ac[("d", ch)] = AC.bump()
        fq_st[("d", ch)] = (FQW[slot], dma(
            "sp", FQW[slot], d8_h[ch * 128:(ch + 1) * 128, :], wd8b_s[slot],
            [(AC, fq_ac[("d", ch)])]))

    gu_chunks = []
    for ch in range(ICN):
        gu_chunks.append(("g", wg_h, g8_h, swg, invg, ch))
        gu_chunks.append(("u", wu_h, u8_h, swu, invu, ch))

    def emit_gu_chunk(gk):
        name, src_h, dst_h, sw, inv, ch = gu_chunks[gk]
        slot = gk % 2
        LSEM = FN[slot]
        waits = [(AC, v_fqd_ac)]
        if gk >= 2:
            pn, pch = gu_chunks[gk - 2][0], gu_chunks[gk - 2][5]
            waits.append((AC, fq_ac[(pn, pch)]))

        ld_v = (LSEM, dma("ac", LSEM, wstg[:, slot, :],
                          src_h[ch * 128:(ch + 1) * 128, :], waits))

        def f(e, ld_v=ld_v, slot=slot, sw=sw, inv=inv, ch=ch):
            e.tensor_reduce(sw[:, ch:ch + 1], wstg[:, slot, :], AX.X,
                            ALU.max, apply_absolute_value=True)
            e.drain()
            e.reciprocal(inv[:, ch:ch + 1], sw[:, ch:ch + 1])
            e.drain()
            e.tensor_scalar_mul(inv[:, ch:ch + 1], inv[:, ch:ch + 1],
                                127.0).then_inc(DV.h, 1)
        em("dv", [ld_v], f)
        v_sc = DV.bump()

        if gk >= 2:
            pn, pch = gu_chunks[gk - 2][0], gu_chunks[gk - 2][5]
            st_prev = fq_st[(pn, pch)]
        else:
            st_prev = (FQN[0], 0)

        def f(e, v_sc=v_sc, slot=slot, inv=inv, ch=ch):
            e.activation(w8stg[:, slot, :], wstg[:, slot, :], AF.Copy,
                         scale=inv[:, ch:ch + 1]).then_inc(AC.h, 1)
        em("ac", [(DV, v_sc), st_prev], f)
        fq_ac[(name, ch)] = AC.bump()
        fq_st[(name, ch)] = (FQN[slot], dma(
            "sp", FQN[slot], dst_h[ch * 128:(ch + 1) * 128, :],
            w8stg[:, slot, :], [(AC, fq_ac[(name, ch)])]))

    wq_ld = {}
    wq_pe = {}
    wq_ac = {}
    wq_dv = {}

    wqs4 = hilo[:].rearrange("p a k n -> p (a k n)").bitcast(F32).rearrange(
        "p (s n) -> p s n", s=4)
    NWQ = MCQ

    def emit_wq(j):
        kc, t = j // 3, j % 3
        slot = j % 4
        waits = [(AC, wq_ac.get(j - 4, 0)), (DV, wq_dv.get(j - 4, 0)),
                 (AC, wq_ac.get(j - 2, 0))]
        wq_ld[j] = (WQ[slot % 2], dma(
            "sp", WQ[slot % 2], wqs4[:, slot, :],
            wq_h[kc * 128:(kc + 1) * 128, t * 1024:(t + 1) * 1024], waits))

        def f(e, j=j, kc=kc, t=t, slot=slot):
            e.activation(whi[:, kc, t * 8:(t + 1) * 8, :], wqs4[:, slot, :],
                         AF.Copy, scale=gamc[:, kc:kc + 1]).then_inc(AC.h, 1)
        em("ac", [wq_ld[j], (DV, v_gc)], f)
        wq_ac[j] = AC.bump()

        def f(e, j=j, kc=kc, t=t, slot=slot):
            e.scalar_tensor_tensor(
                wlo[:, kc, t * 8:(t + 1) * 8, :], wqs4[:, slot, :],
                gamc[:, kc:kc + 1], whi[:, kc, t * 8:(t + 1) * 8, :],
                ALU.mult, ALU.subtract).then_inc(DV.h, 1)
        em("dv", [(PE, 0), (AC, wq_ac[j])], f)
        wq_dv[j] = DV.bump()


    for i in range(NT):
        emit_a1(i)
    fqw_totals = [(FQW[0], FQW[0].v), (FQW[1], FQW[1].v)]
    v_fqd_dv = DV.v
    v_fqd_ac = AC.v

    gu_ptr = [0]
    xld = {}
    xn_dv = {}
    xn_st = {}
    hq_dv = {}
    tr_pe = {}
    hl_ac = {}
    hl_dv = {}
    mm_pe = {}
    ta_ac = {}
    tq_dv = {}
    qy_st = {}

    def emit_sub_prep(i):
        tt, sub = i // 4, i % 4
        hl = hilo if tt % 2 == 0 else hilo2
        if True:
            slot = i % 2
            xld[i] = (XL[slot], dma("gp", XL[slot], xbuf[:, slot, :],
                                    x_h[i * 128:(i + 1) * 128, :],
                                    [(DV, v_row3b[i // 8]), (DV, hq_dv.get(i - 2, 0))]))

            def f(e, i=i, slot=slot):
                e.tensor_scalar(xnb[:, slot, :], xbuf[:, slot, :],
                                muc[:, i:i + 1], rc[:, i:i + 1],
                                ALU.subtract, ALU.mult).then_inc(DV.h, 1)
                e.drain()
                e.tensor_reduce(xmaxc[:, i:i + 1], xnb[:, slot, :], AX.X,
                                ALU.max, apply_absolute_value=True).then_inc(DV.h, 1)
            em("dv", [xld[i], (DV, v_row3b[i // 8]),
                      xn_st.get(i - 2, (XNS[0], 0))], f)
            xn_dv[i] = DV.bump(2)
            xn_st[i] = (XNS[slot], dma("sp", XNS[slot],
                                       xn_h[i * 128:(i + 1) * 128, :],
                                       xnb[:, slot, :], [(DV, xn_dv[i])]))

            def f(e, i=i, slot=slot):
                e.tensor_scalar(hqb[:, slot, :], xbuf[:, slot, :],
                                muc[:, i:i + 1], hscc[:, i:i + 1],
                                ALU.subtract, ALU.mult).then_inc(DV.h, 1)
            em("dv", [(PE, tr_pe.get(i - 2, 0))], f)
            hq_dv[i] = DV.bump()

    def emit_sub_tr(i):
        tt, sub = i // 4, i % 4
        hl = hilo if tt % 2 == 0 else hilo2
        slot = i % 2
        if True:
            def f(e, i=i, slot=slot):
                for kc in range(KC):
                    bank = pb[6] if kc < 4 else pb[7]
                    ins = e.transpose(bank[:, (kc % 4) * 128:(kc % 4 + 1) * 128],
                                      hqb[:, slot, kc * 128:(kc + 1) * 128],
                                      idf[:])
                    if kc in (3, 7):
                        ins.then_inc(PE.h, 1)
            em("pe", [(DV, hq_dv[i]), (AC, hl_ac.get(i - 1, 0)),
                      (DV, hl_dv.get(i - 1, 0))], f)
            tr_pe[i] = PE.bump(2)

            w_extra = (PE, mm_pe.get((tt - 2, 7), 0)) if sub == 0 else (PE, 0)
            w_wq = ((DV, wq_dv[NWQ - 1]), (AC, wq_ac[NWQ - 1])) \
                if (tt == 0 and sub == 0) else ((DV, 0), (AC, 0))
            w_dprep = ((DV, v_fqd_dv) if (tt == 1 and sub == 0) else (DV, 0),
                       (AC, v_fqd_ac) if (tt == 1 and sub == 0) else (AC, 0))


            def f(e, i=i, sub=sub, hl=hl):
                for half in range(2):
                    bank = pb[6 + half]
                    dst = hl[:, 0, half * 4:half * 4 + 4,
                             sub * 128:(sub + 1) * 128]
                    e.activation(dst, bank[:, 0:512],
                                 AF.Copy).then_inc(AC.h, 1)
            em("ac", [(PE, tr_pe[i]), w_extra, w_dprep[1], w_wq[1]], f)
            hl_ac[i] = AC.bump(2)

            def f(e, i=i, sub=sub, hl=hl):
                for half in range(2):
                    bank = pb[6 + half]
                    dst = hl[:, 1, half * 4:half * 4 + 4,
                             sub * 128:(sub + 1) * 128]
                    src_hi = hl[:, 0, half * 4:half * 4 + 4,
                                sub * 128:(sub + 1) * 128]
                    e.tensor_tensor(dst, bank[:, 0:512], src_hi,
                                    ALU.subtract).then_inc(DV.h, 1)
            em("dv", [(PE, tr_pe[i]), (AC, hl_ac[i]), w_extra,
                      w_dprep[0], w_wq[0]], f)
            hl_dv[i] = DV.bump(2)

    def emit_group(tt, g):
        hl = hilo if tt % 2 == 0 else hilo2
        if True:
            ps = g % 2
            banks = (pb[3 * ps], pb[3 * ps + 1], pb[3 * ps + 2])
            prev = tq_dv.get((tt, g - 2), 0) or (
                tq_dv.get((tt - 1, g + 6), 0) if g < 2 else 0)
            preva = ta_ac.get((tt, g - 2), 0) or (
                ta_ac.get((tt - 1, g + 6), 0) if g < 2 else 0)

            def f(e, tt=tt, g=g, banks=banks, hl=hl):
                for bi, mc in enumerate((g, 8 + g, 16 + g)):
                    cnt = 0
                    for wt, xi in ((whi, 0), (whi, 1), (wlo, 0)):
                        for kc in range(KC):
                            ins = e.matmul(banks[bi][:, :], wt[:, kc, mc, :],
                                           hl[:, xi, kc, :],
                                           start=(cnt == 0), stop=(cnt == 23))
                            cnt += 1
                    ins.then_inc(PE.h, 1)
            em("pe", [(AC, hl_ac[tt * 4 + 3]), (DV, hl_dv[tt * 4 + 3]),
                      (DV, prev), (AC, preva)], f)
            mm_pe[(tt, g)] = PE.bump(3)

            gs = g % 2

            def f(e, tt=tt, g=g, banks=banks, gs=gs):
                e.activation(af[:, gs, :], banks[2][:, :],
                             AF.Sigmoid).then_inc(AC.h, 1)
            em("ac", [(PE, mm_pe[(tt, g)]), (DV, prev)], f)
            ta_ac[(tt, g)] = AC.bump()

            def f(e, tt=tt, g=g, banks=banks, gs=gs):
                init = 0.0 if tt == 0 else carry[:, g:g + 1]
                e.tensor_tensor_scan(yf[:, gs, :], af[:, gs, :],
                                     banks[1][:, :], init, ALU.mult, ALU.add)
                e.drain()
                e.tensor_copy(carry[:, g:g + 1], yf[:, gs, 511:512])
                e.tensor_tensor(qyb[:, gs, :], banks[0][:, :], yf[:, gs, :],
                                ALU.mult)
                e.drain().then_inc(DV.h, 1)
            em("dv", [(PE, mm_pe[(tt, g)]), (AC, ta_ac[(tt, g)]),
                      qy_st.get((tt, g - 2), (QYS[0], 0)) if g >= 2 else
                      qy_st.get((tt - 1, g + 6), (QYS[0], 0))], f)
            tq_dv[(tt, g)] = DV.bump()

            qy_st[(tt, g)] = (QYS[g], dma(
                "sp", QYS[g],
                qy_h[g * 128:(g + 1) * 128, tt * 512:(tt + 1) * 512],
                qyb[:, gs, :], [(DV, tq_dv[(tt, g)])]))

    for sub in range(4):
        emit_sub_prep(sub)
        emit_sub_tr(sub)
    for tt in range(NTT):
        for g in range(4):
            emit_group(tt, g)
        if tt + 1 < NTT:
            b = (tt + 1) * 4
            emit_sub_prep(b)
            emit_sub_prep(b + 1)
            emit_sub_tr(b)
        for g in range(4, 8):
            emit_group(tt, g)
            if tt + 1 < NTT:
                b = (tt + 1) * 4
                if g == 4:
                    emit_sub_prep(b + 2)
                    emit_sub_tr(b + 1)
                elif g == 5:
                    emit_sub_prep(b + 3)
                    emit_sub_tr(b + 2)
                elif g == 6:
                    emit_sub_tr(b + 3)
        while gu_ptr[0] < min(len(gu_chunks), (tt + 1) * 7):
            emit_gu_chunk(gu_ptr[0])
            gu_ptr[0] += 1

    vA = {"pe": PE.v, "dv": DV.v, "ac": AC.v}
    qys_totals = [(QYS[g], QYS[g].v) for g in range(8)]
    fq_totals = fqw_totals + [(FQN[0], FQN[0].v), (FQN[1], FQN[1].v)]
    v_fq_dv = DV.v

    def f(e):
        e.tensor_reduce(vmaxr[:], xmaxc[:], AX.X, ALU.max)
        e.drain().then_inc(DV.h, 1)
    em("dv", [(DV, xn_dv[NT - 1])], f)
    v_xm1 = DV.bump()

    em("pe", [(DV, v_xm1), (PE, vA["pe"])],
       lambda e: e.transpose(pb[6][0:1, 0:128], vmaxr[:, 0:1],
                             idf[:]).then_inc(PE.h, 1))
    v_xm2 = PE.bump()

    def f(e):
        e.memset(smt[:], 0.0)
        e.drain()
        e.tensor_reduce(smt[:, 0:1], pb[6][0:1, 0:128], AX.X, ALU.max)
        e.drain().then_inc(DV.h, 1)
    em("dv", [(PE, v_xm2)], f)
    v_xm3 = DV.bump()

    cc1w = dma("gp", CC, cc1i.ap(), smt[:], [(DV, v_xm3)])

    def f(e):
        e.collective_compute(
            "AllReduce", ALU.max, replica_groups=[list(range(8))],
            ins=[cc1i.ap().opt()], outs=[cc1o.ap().opt()],
        ).then_inc(CCE.h, 1)
    em("gp", [(CC, cc1w)], f)
    v_cc1 = CCE.bump()

    gu_ld = {}
    gu_cast = {}
    gu_pe = {}
    gu_dv = {}
    order = [("g", g8_h, qwg), ("u", u8_h, qwu)]
    for oi, (name, src8, dstw) in enumerate(order):
        for ic in range(ICN):
            k = oi * ICN + ic
            slot = k % 2
            gu_ld[k] = (GUL[slot], dma(
                "gp", GUL[slot], w8stg[:, slot, :],
                src8[ic * 128:(ic + 1) * 128, :],
                fq_totals + [(DV, gu_cast.get(k - 2, 0))]))

            def f(e, k=k, slot=slot):
                e.tensor_copy(wbstg[:, slot, :], w8stg[:, slot, :])
                e.drain().then_inc(DV.h, 1)
            em("dv", [gu_ld[k],
                      (PE, mm_pe[(NTT - 1, 7)]),
                      (DV, gu_dv.get(k - 2, 0))], f)
            gu_cast[k] = DV.bump()

            def f(e, k=k, slot=slot):
                for kc in range(KC):
                    bank = pb[6] if kc < 4 else pb[7]
                    ins = e.transpose(bank[:, (kc % 4) * 128:(kc % 4 + 1) * 128],
                                      wbstg[:, slot, kc * 128:(kc + 1) * 128],
                                      idf[:])
                    if kc in (3, 7):
                        ins.then_inc(PE.h, 1)
            em("pe", [(DV, gu_cast[k]), (PE, vA["pe"]),
                      (DV, gu_dv.get(k - 1, 0)), (DV, v_xm3 if k == 0 else 0)], f)
            gu_pe[k] = PE.bump(2)

            def f(e, k=k, name=name, ic=ic, dstw=dstw):
                for half in range(2):
                    bank = pb[6 + half]
                    dst = dstw[:, half * 4:half * 4 + 4, ic, :]
                    e.tensor_copy(dst, bank[:, 0:512]).then_inc(DV.h, 1)
            em("dv", [(PE, gu_pe[k])], f)
            gu_dv[k] = DV.bump(2)
    v_gu_dv = DV.v

    cc1r = dma("gp", CC, smt2[:], cc1o.ap(), [(CCE, v_cc1)])

    em("ac", [(CST, w_cst)],
       lambda e: e.activation(s1s2[:, 2:4], clip_t[:, 0:2],
                              AF.Sigmoid).then_inc(AC.h, 1))
    v_sig = AC.bump()

    def f(e):
        e.tensor_mul(smt2[:, 2:3], smt2[:, 0:1], s1s2[:, 2:3])
        e.drain()
        e.tensor_scalar_mul(smt2[:, 2:3], smt2[:, 2:3], QMAXI)
        e.drain()
        e.reciprocal(smt2[:, 1:2], smt2[:, 2:3])
        e.drain()
        e.tensor_scalar_mul(smt2[:, 2:3], smt2[:, 2:3], QMAXI)
        e.drain().then_inc(DV.h, 1)
    em("dv", [(CC, cc1r), (AC, v_sig)], f)
    v_s1 = DV.bump()

    em("pe", [(DV, v_s1), (DV, v_ones), (PE, vA["pe"])],
       lambda e: e.matmul(pb[0][:, 0:2], ones_r[:, :], smt2[0:1, 1:3],
                          start=True, stop=True).then_inc(PE.h, 1))
    v_bc1 = PE.bump()

    def f(e):
        e.tensor_copy(inv1c[:], pb[0][:, 0:1])
        e.tensor_copy(s1col[:], pb[0][:, 1:2])
        e.drain()
        e.tensor_scalar(sgate[:], swg[:], s1col[:, 0:1], None,
                        ALU.mult)
        e.drain()
        e.tensor_mul(comb[:], sgate[:], swu[:])
        e.drain()
        e.tensor_scalar(comb[:], comb[:], s1col[:, 0:1], None, ALU.mult)
        e.drain().then_inc(DV.h, 1)
    em("dv", [(PE, v_bc1), (DV, v_fq_dv)], f)
    v_bc1c = DV.bump()

    xnld = {}
    xq_ac = {}
    xq_dv = {}
    xqt_pe = {}
    xqe_dv = {}
    gmm_pe = {}
    sg_ac = {}
    vv_dv = {}
    vv_gp = {}
    vmx_dv = {}
    vst = {}
    upf = qf
    bu_of = {ic: pb[2 + (ic % 2)] for ic in range(ICN)}

    def emit_xqp_prep(i):
        tt, sub = i // 4, i % 4
        xqf = xqfm if tt % 2 == 0 else xqfm2
        if True:
            slot = i % 2
            xnld[i] = (XL[slot], dma(
                "gp", XL[slot], xbuf[:, slot, :],
                xn_h[i * 128:(i + 1) * 128, :],
                [xn_st[i], (DV, v_bc1c), (AC, xq_ac.get(i - 2, 0))]))

            def f(e, i=i, slot=slot):
                e.activation(xq8[:, slot, :], xbuf[:, slot, :], AF.Copy,
                             scale=inv1c[:, 0:1]).then_inc(AC.h, 1)
            em("ac", [xnld[i], (DV, v_bc1c),
                      (DV, xq_dv.get(i - 2, 0)), (AC, vA["ac"])], f)
            xq_ac[i] = AC.bump()

            def f(e, i=i, slot=slot):
                e.tensor_copy(xnb[:, slot, :],
                              xq8[:, slot, :]).then_inc(DV.h, 1)
            em("dv", [(AC, xq_ac[i]),
                      (PE, xqt_pe.get(i - 2, 0)),
                      xn_st[NT - 1] if i < 2 else (XNS[0], 0),
                      xn_st[NT - 2] if i < 2 else (XNS[1], 0)], f)
            xq_dv[i] = DV.bump()

    def emit_xqp_tr(i):
        tt, sub = i // 4, i % 4
        xqf = xqfm if tt % 2 == 0 else xqfm2
        slot = i % 2
        if True:
            def f(e, i=i, slot=slot):
                for kc in range(KC):
                    bank = pb[4] if kc < 4 else pb[5]
                    ins = e.transpose(bank[:, (kc % 4) * 128:(kc % 4 + 1) * 128],
                                      xnb[:, slot, kc * 128:(kc + 1) * 128],
                                      idf[:])
                    if kc in (3, 7):
                        ins.then_inc(PE.h, 1)
            em("pe", [(DV, xq_dv[i]),
                      (DV, xqe_dv.get(i - 1, 0))], f)
            xqt_pe[i] = PE.bump(2)

            wx = (PE, gmm_pe.get((tt - 2, ICN - 1), 0)) if sub == 0 else (PE, 0)

            def f(e, i=i, sub=sub, xqf=xqf):
                for half in range(2):
                    bank = pb[4 + half]
                    dst = xqf[:, half * 4:half * 4 + 4,
                              sub * 128:(sub + 1) * 128]
                    e.tensor_copy(dst, bank[:, 0:512]).then_inc(DV.h, 1)
            em("dv", [(PE, xqt_pe[i]), wx], f)
            xqe_dv[i] = DV.bump(2)

    def emit_ic(tt, ic):
        xqf = xqfm if tt % 2 == 0 else xqfm2
        if True:
            gs = ic % 2
            bg, bu = pb[gs], pb[2 + gs]
            prev = vv_dv.get((tt, ic - 2), 0) or (
                vv_dv.get((tt - 1, ICN - 2 + ic), 0) if ic < 2 else 0)

            def f(e, tt=tt, ic=ic, bg=bg, bu=bu, xqf=xqf):
                for kc in range(KC):
                    ins = e.matmul(bg[:, :], qwg[:, kc, ic, :], xqf[:, kc, :],
                                   start=kc == 0, stop=kc == KC - 1)
                ins.then_inc(PE.h, 1)
                for kc in range(KC):
                    ins = e.matmul(bu[:, :], qwu[:, kc, ic, :],
                                   xqf[:, kc, :],
                                   start=kc == 0, stop=kc == KC - 1)
                ins.then_inc(PE.h, 1)
            em("pe", [(DV, xqe_dv[tt * 4 + 3]), (DV, prev),
                      (AC, sg_ac.get((tt, ic - 2), 0)),
                      (DV, gu_dv[2 * ic + 1] if tt == 0 else 0),
                      (DV, v_bc1c if (tt == 0 and ic == 0) else 0)], f)
            gmm_pe[(tt, ic)] = PE.bump(2)

            def f(e, tt=tt, ic=ic, bg=bg, gs=gs):
                e.activation(sigb[:, gs, :], bg[:, :], AF.Sigmoid,
                             scale=sgate[:, ic:ic + 1]).then_inc(AC.h, 1)
            em("ac", [(PE, gmm_pe[(tt, ic)]), (DV, prev),
                      (DV, vA["dv"] if (tt == 0 and ic < 2) else 0)], f)
            sg_ac[(tt, ic)] = AC.bump()

            half = ic // 11
            icc = ic % 11
            if icc == 0:
                if tt == 0 and half == 0:
                    w_vfree = (AC, v_fqd_ac)
                else:
                    prev_half = (tt, 0) if half == 1 else (tt - 1, 1)
                    w_vfree = (VS, vst[prev_half])
            else:
                w_vfree = (VS, 0)

            stt_v = DV.v + 1

            def f(e, tt=tt, ic=ic, bg=bg, bu=bu, gs=gs, icc=icc, stt_v=stt_v):
                e.scalar_tensor_tensor(vtmp[:, gs, :], sigb[:, gs, :],
                                       comb[:, ic:ic + 1], bg[:, :],
                                       ALU.mult, ALU.mult).then_inc(DV.h, 1)
                e.wait_ge(DV.h, stt_v)
                e.tensor_tensor(vstage[:, icc, :], vtmp[:, gs, :],
                                bu[:, :], ALU.mult).then_inc(DV.h, 1)
            em("dv", [(PE, gmm_pe[(tt, ic)]), (AC, sg_ac[(tt, ic)]),
                      w_vfree,
                      (DV, vmx_dv.get((tt, half - 1) if half else
                                      (tt - 1, 1), 0))], f)
            DV.bump()
            vv_dv[(tt, ic)] = DV.bump()
            vv_gp[(tt, ic)] = 0

            if icc == 10:
                def f(e, tt=tt, half=half):
                    e.tensor_reduce(vmaxc[:, half:half + 1],
                                    vstage[:].rearrange("p a b -> p (a b)"),
                                    AX.X, ALU.max, apply_absolute_value=True)
                    e.drain()
                    if tt == 0 and half == 0:
                        e.tensor_copy(vmaxrun[:, 0:1], vmaxc[:, 0:1])
                    else:
                        e.tensor_tensor(vmaxrun[:, 0:1], vmaxrun[:, 0:1],
                                        vmaxc[:, half:half + 1], ALU.max)
                    e.drain().then_inc(DV.h, 1)
                em("dv", [(DV, vv_dv[(tt, ic)])], f)
                vmx_dv[(tt, half)] = DV.bump()

                vst[(tt, half)] = dma(
                    "sp", VS,
                    v_h.rearrange("(ic p) n -> p ic n", p=128)
                    [:, half * 11:(half + 1) * 11, tt * 512:(tt + 1) * 512],
                    vstage[:, 0:11, :], [(DV, vv_dv[(tt, ic)])])

    for sub in range(4):
        emit_xqp_prep(sub)
        emit_xqp_tr(sub)
    for tt in range(NTT):
        for ic in range(11):
            emit_ic(tt, ic)
        if tt + 1 < NTT:
            b = (tt + 1) * 4
            emit_xqp_prep(b)
            emit_xqp_prep(b + 1)
            emit_xqp_tr(b)
        for ic in range(11, ICN):
            emit_ic(tt, ic)
            if tt + 1 < NTT:
                b = (tt + 1) * 4
                if ic == 11:
                    emit_xqp_prep(b + 2)
                    emit_xqp_tr(b + 1)
                elif ic == 12:
                    emit_xqp_prep(b + 3)
                    emit_xqp_tr(b + 2)
                elif ic == 13:
                    emit_xqp_tr(b + 3)

    vB = {"pe": PE.v, "dv": DV.v, "ac": AC.v, "gp": GP.v}

    def f(e):
        e.tensor_copy(vmaxr[:], vmaxrun[:, 0:1])
        e.drain().then_inc(DV.h, 1)
    em("dv", [(DV, vmx_dv[(NTT - 1, 1)])], f)
    v_vm1 = DV.bump()

    em("pe", [(DV, v_vm1), (PE, vB["pe"])],
       lambda e: e.transpose(pb[6][0:1, 0:128], vmaxr[:, 0:1],
                             idf[:]).then_inc(PE.h, 1))
    v_vm2 = PE.bump()

    def f(e):
        e.memset(smt[:], 0.0)
        e.drain()
        e.tensor_reduce(smt[:, 0:1], pb[6][0:1, 0:128], AX.X, ALU.max)
        e.drain().then_inc(DV.h, 1)
    em("dv", [(PE, v_vm2)], f)
    v_vm3 = DV.bump()

    cc2w = dma("gp", CC, cc2i.ap(), smt[:], [(DV, v_vm3)])

    def f(e):
        e.collective_compute(
            "AllReduce", ALU.max, replica_groups=[list(range(8))],
            ins=[cc2i.ap().opt()], outs=[cc2o.ap().opt()],
        ).then_inc(CCE.h, 1)
    em("gp", [(CC, cc2w)], f)
    v_cc2 = CCE.bump()

    d_ld = {}
    d_cast = {}
    d_pe = {}
    d_dv = {}
    NGRP = 6
    for hc in range(HCN):
        slot = hc % 2
        d_ld[hc] = (DL[slot], dma(
            "gp", DL[slot], wd8b_s[slot],
            d8_h[hc * 128:(hc + 1) * 128, :],
            [(PE, vB["pe"]), (DV, vB["dv"]), (AC, vB["ac"]),
             (GP, vB["gp"]), (DV, d_cast.get(hc - 2, 0))]))

        def f(e, hc=hc, slot=slot):
            e.tensor_copy(wdstg[:, slot, :], wd8b_s[slot])
            e.drain().then_inc(DV.h, 1)
        em("dv", [d_ld[hc], (VS, vst[(NTT - 1, 1)]),
                  (DV, d_dv.get((hc - 2, NGRP - 1), 0)),
                  (DV, v_vm3 if hc == 0 else 0)], f)
        d_cast[hc] = DV.bump()

        for grp in range(NGRP):
            lo_ic = grp * 4
            hi_ic = min(lo_ic + 4, ICN)

            def f(e, hc=hc, slot=slot, lo_ic=lo_ic, hi_ic=hi_ic, grp=grp):
                bank = pb[6 + grp % 2]
                for j, icb in enumerate(range(lo_ic, hi_ic)):
                    ins = e.transpose(bank[:, j * 128:(j + 1) * 128],
                                      wdstg[:, slot,
                                            icb * 128:(icb + 1) * 128],
                                      idf[:])
                ins.then_inc(PE.h, 1)
            em("pe", [(DV, d_cast[hc]), (PE, vB["pe"]),
                      (DV, d_dv.get((hc, grp - 2), 0) or
                       (d_dv.get((hc - 1, NGRP - 2 + grp), 0)
                        if grp < 2 else 0)),
                      (DV, v_vm3 if (hc == 0 and grp < 2) else 0)], f)
            d_pe[(hc, grp)] = PE.bump()

            def f(e, hc=hc, grp=grp, lo_ic=lo_ic, hi_ic=hi_ic):
                bank = pb[6 + grp % 2]
                e.tensor_copy(qwd[:, lo_ic:hi_ic, hc, :],
                              bank[:, 0:(hi_ic - lo_ic) * 128]).then_inc(DV.h, 1)
            em("dv", [(PE, d_pe[(hc, grp)])], f)
            d_dv[(hc, grp)] = DV.bump()
    v_d_dv = DV.v

    cc2r = dma("gp", CC, smt2[:, 4:6], cc2o.ap()[0:1, 0:2], [(CCE, v_cc2)])

    def f(e):
        e.tensor_mul(smt2[:, 5:6], smt2[:, 4:5], s1s2[:, 3:4])
        e.drain()
        e.tensor_scalar_mul(smt2[:, 5:6], smt2[:, 5:6], QMAXI)
        e.drain()
        e.reciprocal(smt2[:, 6:7], smt2[:, 5:6])
        e.drain()
        e.tensor_scalar_mul(smt2[:, 5:6], smt2[:, 5:6], QMAXI)
        e.drain().then_inc(DV.h, 1)
    em("dv", [(CC, cc2r), (AC, v_sig)], f)
    v_s2 = DV.bump()

    em("pe", [(DV, v_s2), (PE, vB["pe"])],
       lambda e: e.matmul(pb[0][:, 0:2], ones_r[:, :], smt2[0:1, 5:7],
                          start=True, stop=True).then_inc(PE.h, 1))
    v_bc2 = PE.bump()

    def f(e):
        e.tensor_copy(s2col[:], pb[0][:, 0:1])
        e.tensor_copy(inv2c[:], pb[0][:, 1:2])
        e.drain()
        e.tensor_scalar(sdown[:], swd[:], s2col[:, 0:1], None, ALU.mult)
        e.drain().then_inc(DV.h, 1)
    em("dv", [(PE, v_bc2)], f)
    v_bc2c = DV.bump()

    vld = {}
    h2_ac = {}
    h2_dv = {}
    dmm_pe = {}
    ds_dv = {}
    otr_pe = {}
    oev_dv = {}
    out_st = {}
    qyl = {}
    xld3 = {}

    qyv = {}
    for hc in range(4):
        qyv[hc] = xnb[:].rearrange("p a b -> p (a b)").rearrange(
            "p (h n) -> p h n", h=4)[:, hc, :]
    qyv[4] = af[:, 0, :]
    qyv[5] = af[:, 1, :]
    qyv[6] = yf[:, 0, :]
    qyv[7] = yf[:, 1, :]
    qyld4 = xnb[:].rearrange("p a b -> p (a b)").rearrange(
        "p (h n) -> p h n", h=4)

    h2q_ac = {}
    h2c_dv = {}

    def emit_cquant(tt, half):
        h2b = h2b2[tt % 2]
        if True:
            prev_h = (tt, 0) if half == 1 else (tt - 1, 1)
            for qtr, (qlo, qhi) in enumerate(((0, 6), (6, 11))):
                w_free = [(VS, vst[(tt, half)]), (DV, v_d_dv)]
                qk = (prev_h[0], prev_h[1], 5 if qtr == 0 else 10)
                if qk in h2q_ac:
                    w_free.append((AC, h2q_ac[qk]))
                vld[(tt, half, qtr)] = (VL[qtr], dma(
                    "gp", VL[qtr], vstage[:, qlo:qhi, :],
                    v_h.rearrange("(ic p) n -> p ic n", p=128)
                    [:, half * 11 + qlo:half * 11 + qhi,
                     tt * 512:(tt + 1) * 512],
                    w_free))

            for icc in range(11):
                ic = half * 11 + icc

                def f(e, tt=tt, half=half, icc=icc, ic=ic):
                    qs = icc % 2
                    off = (icc % 4 // 2) * 512
                    e.activation(h28[:, qs, off:off + 512], vstage[:, icc, :],
                                 AF.Copy,
                                 scale=inv2c[:, 0:1]).then_inc(AC.h, 1)
                em("ac", [vld[(tt, half, 0 if icc < 6 else 1)], (DV, v_bc2c),
                          (AC, vB["ac"] if (tt == 0 and half == 0) else 0),
                          (DV, h2c_dv.get((tt, half, icc - 4), 0) or
                           (h2c_dv.get(prev_h + (10 - ((10 - icc) % 4),), 0)
                            if icc < 4 else 0))], f)
                h2q_ac[(tt, half, icc)] = AC.bump()

                def f(e, tt=tt, half=half, icc=icc, ic=ic, h2b=h2b):
                    qs = icc % 2
                    off = (icc % 4 // 2) * 512
                    e.tensor_copy(h2b[:, ic, :],
                                  h28[:, qs, off:off + 512]).then_inc(DV.h, 1)
                em("dv", [(AC, h2q_ac[(tt, half, icc)]),
                          (DV, vB["dv"] if (tt == 0 and half == 0
                                            and icc == 0) else 0),
                          (PE, dmm_pe.get((tt - 2, HCN - 1), 0)
                           if (half == 0 and icc == 0) else 0)], f)
                h2c_dv[(tt, half, icc)] = DV.bump()
            h2_dv[(tt, half)] = h2c_dv[(tt, half, 10)]

    def emit_qyl(tt):
        qyl[tt] = (QYL, dma("gp", QYL, qyld4[:, :, :],
                            qy_h.rearrange("(g p) n -> p g n", p=128)
                            [:, 0:4, tt * 512:(tt + 1) * 512],
                            qys_totals + [(PE, vB["pe"]), (AC, vB["ac"]),
                            (GP, vB["gp"]),
                            (DV, ds_dv.get((tt - 1, HCN - 1), 0))]))
        qyl2 = dma("gp", QYL, af[:, :, :],
                   qy_h.rearrange("(g p) n -> p g n", p=128)
                   [:, 4:6, tt * 512:(tt + 1) * 512], [])
        qyl3 = dma("gp", QYL, yf[:, :, :],
                   qy_h.rearrange("(g p) n -> p g n", p=128)
                   [:, 6:8, tt * 512:(tt + 1) * 512], [])
        qyl[tt] = (QYL, qyl3)

    def emit_dmm(tt, hc):
        h2b = h2b2[tt % 2]
        if True:
            gs = hc % 2
            bank = pb[gs]

            def f(e, tt=tt, hc=hc, bank=bank, h2b=h2b):
                for ic in range(ICN):
                    ins = e.matmul(bank[:, :], qwd[:, ic, hc, :],
                                   h2b[:, ic, :],
                                   start=ic == 0, stop=ic == ICN - 1)
                ins.then_inc(PE.h, 1)
            em("pe", [(DV, h2_dv[(tt, 1)]),
                      (DV, ds_dv.get((tt, hc - 2), 0) or
                       (ds_dv.get((tt - 1, HCN - 2 + hc), 0) if hc < 2 else 0)),
                      (DV, v_bc2c if (tt == 0 and hc == 0) else 0)], f)
            dmm_pe[(tt, hc)] = PE.bump()

            def f(e, tt=tt, hc=hc, bank=bank):
                e.scalar_tensor_tensor(outsum[:, hc, :], bank[:, :],
                                       sdown[:, hc:hc + 1], qyv[hc],
                                       ALU.mult, ALU.add)
                e.drain().then_inc(DV.h, 1)
            em("dv", [(PE, dmm_pe[(tt, hc)]), qyl[tt],
                      (PE, otr_pe.get((tt - 1, 3), 0) if hc < 2 else 0)], f)
            ds_dv[(tt, hc)] = DV.bump()

    def emit_cout(tt, sub):
        if True:
            i = tt * 4 + sub
            slot = i % 2
            xld3[i] = (XL[slot], dma(
                "gp", XL[slot], xbuf[:, slot, :],
                x_h[i * 128:(i + 1) * 128, :],
                [(AC, vB["ac"]), (DV, oev_dv.get(i - 2, 0))]))

            base = 6 if sub % 2 == 0 else 4

            def f(e, tt=tt, sub=sub, base=base):
                for hc in range(HCN):
                    bank = pb[base] if hc < 4 else pb[base + 1]
                    ins = e.transpose(bank[:, (hc % 4) * 128:(hc % 4 + 1) * 128],
                                      outsum[:, hc, sub * 128:(sub + 1) * 128],
                                      idf[:])
                    if hc in (3, 7):
                        ins.then_inc(PE.h, 1)
            em("pe", [(DV, ds_dv[(tt, HCN - 1)]),
                      (DV, oev_dv.get(tt * 4 + sub - 2, 0)),
                      (PE, v_bc2 if (tt == 0 and sub == 0) else 0),
                      (DV, vB["dv"] if (tt == 0 and sub == 1) else 0)], f)
            otr_pe[(tt, sub)] = PE.bump(2)

            def f(e, i=i, tt=tt, sub=sub, slot=slot, base=base):
                for half in range(2):
                    bank = pb[base + half]
                    e.tensor_tensor(
                        outtm[:, slot, half * 512:(half + 1) * 512],
                        bank[:, 0:512],
                        xbuf[:, slot, half * 512:(half + 1) * 512],
                        ALU.add).then_inc(DV.h, 1)
            em("dv", [(PE, otr_pe[(tt, sub)]), xld3[i],
                      out_st.get(i - 2, (OS[0], 0))], f)
            oev_dv[i] = DV.bump(2)

            out_st[i] = (OS[slot], dma("sp", OS[slot],
                                       out_h[i * 128:(i + 1) * 128, :],
                                       outtm[:, slot, :], [(DV, oev_dv[i])]))

    emit_cquant(0, 0)
    emit_cquant(0, 1)
    for tt in range(NTT):
        emit_qyl(tt)
        for hc in range(4):
            emit_dmm(tt, hc)
        if tt + 1 < NTT:
            emit_cquant(tt + 1, 0)
            emit_cquant(tt + 1, 1)
        for hc in range(4, HCN):
            emit_dmm(tt, hc)
        for sub in range(4):
            emit_cout(tt, sub)

    em("sp", [(OS[0], OS[0].v), (OS[1], OS[1].v)], lambda e: e.nop())

    with nc.Block() as block:
        @block.sync
        def _(e):
            for f_ in ops["sp"]:
                f_(e)

        @block.gpsimd
        def _(e):
            for f_ in ops["gp"]:
                f_(e)

        @block.tensor
        def _(e):
            for f_ in ops["pe"]:
                f_(e)

        @block.vector
        def _(e):
            for f_ in ops["dv"]:
                f_(e)

        @block.scalar
        def _(e):
            for f_ in ops["ac"]:
                f_(e)

    return nc


_NC_CACHE = None


def kernel(L0_attn_x, L0_attn_w_qkva, L0_attn_gamma, L0_ffn_clip_x,
           L0_ffn_clip_down, L0_ffn_w_gate, L0_ffn_w_up, L0_ffn_w_down):
    global _NC_CACHE
    if _NC_CACHE is None:
        _NC_CACHE = build()
    nc = _NC_CACHE

    x = np.ascontiguousarray(np.asarray(L0_attn_x, dtype=np.float32))
    wq = np.ascontiguousarray(np.asarray(L0_attn_w_qkva, dtype=np.float32).T)
    gam = np.ascontiguousarray(
        np.asarray(L0_attn_gamma, dtype=np.float32).reshape(8, 128))
    wg = np.ascontiguousarray(np.asarray(L0_ffn_w_gate, dtype=np.float32))
    wu = np.ascontiguousarray(np.asarray(L0_ffn_w_up, dtype=np.float32))
    wd = np.ascontiguousarray(np.asarray(L0_ffn_w_down, dtype=np.float32))
    clips = np.stack([np.asarray(L0_ffn_clip_x, dtype=np.float32).ravel()[0],
                      np.asarray(L0_ffn_clip_down,
                                 dtype=np.float32).ravel()[0]]).reshape(1, 2)
    idf = np.eye(128, dtype=np.float32)

    in_maps = []
    for c in range(8):
        in_maps.append({
            "x": np.ascontiguousarray(x[c]),
            "w_qkva": wq, "gamma": gam, "w_gate": wg, "w_up": wu,
            "w_down": wd, "clips": clips, "ident_f": idf,
        })
    res = run_bass_kernel_spmd(nc, in_maps, core_ids=list(range(8)))
    out = np.stack([res.results[c]["out"] for c in range(8)], axis=0)
    return out


# version 21
# speedup vs baseline: 1.0193x; 1.0193x over previous
import numpy as np
import ml_dtypes
import concourse.bass as bass
import concourse.mybir as mybir
from concourse.bass_utils import run_bass_kernel_spmd

F32 = mybir.dt.float32
BF16 = mybir.dt.bfloat16
I8 = mybir.dt.int8
AF = mybir.ActivationFunctionType
ALU = mybir.AluOpType
AX = mybir.AxisListType

D = 1024
N = 4096
E = 3072
IF = 2816
KC = 8
MCQ = 24
ICN = 22
HCN = 8
NT = 32
NTT = 8
QMAXI = 1.0 / 127.0


def build():
    nc = bass.Bass("TRN2", target_bir_lowering=False)

    x_h = nc.dram_tensor("x", [N, D], F32, kind="ExternalInput").ap()
    wq_h = nc.dram_tensor("w_qkva", [D, E], F32, kind="ExternalInput").ap()
    gam_h = nc.dram_tensor("gamma", [8, 128], F32, kind="ExternalInput").ap()
    wg_h = nc.dram_tensor("w_gate", [IF, D], F32, kind="ExternalInput").ap()
    wu_h = nc.dram_tensor("w_up", [IF, D], F32, kind="ExternalInput").ap()
    wd_h = nc.dram_tensor("w_down", [D, IF], F32, kind="ExternalInput").ap()
    clip_h = nc.dram_tensor("clips", [1, 2], F32, kind="ExternalInput").ap()
    idf_h = nc.dram_tensor("ident_f", [128, 128], F32, kind="ExternalInput").ap()
    out_h = nc.dram_tensor("out", [N, D], F32, kind="ExternalOutput").ap()
    xn_h = nc.dram_tensor("xn_sp", [N, D], F32).ap()
    qy_h = nc.dram_tensor("qy_sp", [D, N], F32).ap()
    v_h = nc.dram_tensor("v_sp", [IF, N], F32).ap()
    g8_h = nc.dram_tensor("g8_sp", [IF, D], I8).ap()
    u8_h = nc.dram_tensor("u8_sp", [IF, D], I8).ap()
    d8_h = nc.dram_tensor("d8_sp", [D, IF], I8).ap()
    cc1i = nc.dram_tensor("cc1i", [1, 8], F32)
    cc1o = nc.dram_tensor("cc1o", [1, 8], F32)
    cc2i = nc.dram_tensor("cc2i", [1, 8], F32)
    cc2o = nc.dram_tensor("cc2o", [1, 8], F32)

    A = nc.alloc_sbuf_tensor
    idf = A("idf", [128, 128], F32)
    gamt = A("gamt", [8, 128], F32)
    gamc = A("gamc", [128, 8], F32)
    clip_t = A("clip_t", [1, 2], F32)
    ones_r = A("ones_r", [1, 128], F32)
    bnst = A("bnst", [128, NT * 12], F32)
    aggr = A("aggr", [128, NT * 2], F32)
    muc = A("muc", [128, NT], F32)
    varc = A("varc", [128, NT], F32)
    rc = A("rc", [128, NT], F32)
    hscc = A("hscc", [128, NT], F32)
    rtmp = A("rtmp", [128, NT], F32)
    xmaxc = A("xmaxc", [128, NT], F32)
    vmaxc = A("vmaxc", [128, ICN], F32)
    vmaxrun = A("vmaxrun", [128, ICN], F32)
    vmaxr = A("vmaxr", [128, 1], F32)
    swg = A("swg", [128, ICN], F32)
    swu = A("swu", [128, ICN], F32)
    swd = A("swd", [128, HCN], F32)
    invg = A("invg", [128, ICN], F32)
    invu = A("invu", [128, ICN], F32)
    invd = A("invd", [128, HCN], F32)
    sgate = A("sgate", [128, ICN], F32)
    comb = A("comb", [128, ICN], F32)
    sdown = A("sdown", [128, HCN], F32)
    smt = A("smt", [1, 8], F32)
    smt2 = A("smt2", [1, 8], F32)
    inv1c = A("inv1c", [128, 1], F32)
    inv2c = A("inv2c", [128, 1], F32)
    s1col = A("s1col", [128, 1], F32)
    s2col = A("s2col", [128, 1], F32)
    s1s2 = A("s1s2", [1, 4], F32)
    carry = A("carry", [128, 8], F32)

    wbufA = A("wbufA", [128, 24576], BF16)
    wbufB = A("wbufB", [128, 24576], BF16)
    whi = wbufA[:].rearrange("p (kc mc f) -> p kc mc f", kc=KC, mc=MCQ)
    wlo = wbufB[:].rearrange("p (kc mc f) -> p kc mc f", kc=KC, mc=MCQ)
    qwg = wbufA[:, 0:KC * ICN * 128].rearrange(
        "p (kc ic f) -> p kc ic f", kc=KC, ic=ICN)
    qwu = wbufB[:, 0:KC * ICN * 128].rearrange(
        "p (kc ic f) -> p kc ic f", kc=KC, ic=ICN)
    qwd = wbufA[:, 0:ICN * HCN * 128].rearrange(
        "p (ic hc f) -> p ic hc f", ic=ICN, hc=HCN)
    h2b2 = [wbufB[:, 0:ICN * 512].rearrange("p (ic n) -> p ic n", ic=ICN),
            wbufB[:, ICN * 512:2 * ICN * 512].rearrange(
                "p (ic n) -> p ic n", ic=ICN)]

    wstg = A("wstg", [128, 2, 1024], F32)
    xqfm = wstg[:].rearrange("p a b -> p (a b)").bitcast(BF16).rearrange(
        "p (kc n) -> p kc n", kc=KC)
    w8stg = A("w8stg", [128, 2, 1024], I8)
    wdstg = A("wdstg", [128, 2, 2816], F32)
    vstage = wdstg[:].rearrange("p a b -> p (a b)").rearrange(
        "p (ic n) -> p ic n", ic=11)
    hilo2 = wdstg[:].rearrange("p a b -> p (a b)").bitcast(BF16)[
        :, 0:2 * KC * 512].rearrange(
        "p (a k n) -> p a k n", a=2, k=KC)

    xbuf = A("xbuf", [128, 2, 1024], F32)
    a1buf = A("a1buf", [128, 2, 1024], F32)
    xqfm2 = a1buf[:].rearrange("p a b -> p (a b)").bitcast(BF16).rearrange(
        "p (kc n) -> p kc n", kc=KC)
    xnb = A("xnb", [128, 2, 1024], F32)
    hqb = A("hqb", [128, 2, 1024], F32)
    outtm = hqb
    hilo = A("hilo", [128, 2, KC, 512], BF16)
    outsum = hilo[:].rearrange("p a k n -> p (a k n)").bitcast(F32).rearrange(
        "p (hc n) -> p hc n", hc=HCN)
    wbstg = hilo[:].rearrange("p a k n -> p (a k n)")[:, 0:4096].bitcast(
        F32).rearrange("p (a b) -> p a b", a=2)
    qf = A("qf", [128, 2, 512], F32)
    kvf = A("kvf", [128, 2, 512], F32)
    wd8b_s = [qf[:].rearrange("p a n -> p (a n)").bitcast(I8)[:, 0:2816],
              kvf[:].rearrange("p a n -> p (a n)").bitcast(I8)[:, 0:2816]]
    af = A("af", [128, 2, 512], F32)
    sigb = af
    yf = A("yf", [128, 2, 512], F32)
    vtmp = yf
    qyb = A("qyb", [128, 2, 512], F32)
    xq8 = A("xq8", [128, 2, 1024], I8)
    h28 = xq8

    pb = [nc.alloc_psum_tensor(f"pb{i}", [128, 512], F32) for i in range(8)]

    class Sem:
        def __init__(self, name):
            self.h = nc.alloc_semaphore(name=name)
            self.v = 0

        def bump(self, n=1):
            self.v += n
            return self.v

    CST = Sem("cst")
    WQ = [Sem("wq0"), Sem("wq1")]
    FN = [Sem("fn0"), Sem("fn1")]
    FW = [Sem("fw0"), Sem("fw1")]
    FQN = [Sem("fqn0"), Sem("fqn1")]
    FQW = [Sem("fqw0"), Sem("fqw1")]
    XL = [Sem("xl0"), Sem("xl1")]
    A1L = [Sem("a1l0"), Sem("a1l1")]
    XNS = [Sem("xns0"), Sem("xns1")]
    QYS = [Sem(f"qys{g}") for g in range(8)]
    VS = Sem("vs")
    OS = [Sem("os0"), Sem("os1")]
    GUL = [Sem("gul0"), Sem("gul1")]
    DL = [Sem("dl0"), Sem("dl1")]
    VL = [Sem("vl0"), Sem("vl1")]
    QYL = Sem("qyl")
    CC = Sem("cc")
    CCE = Sem("cce")
    PE = Sem("pe")
    DV = Sem("dv")
    AC = Sem("ac")
    GP = Sem("gp")

    ops = {"sp": [], "gp": [], "pe": [], "dv": [], "ac": []}

    def em(eng, waits, fn):
        waits = tuple(waits)
        def g(e, waits=waits, fn=fn):
            for (s, v) in waits:
                if v > 0:
                    e.wait_ge(s.h, v)
            fn(e)
        ops[eng].append(g)

    def dma(eng, sem, out_ap, in_ap, waits=()):
        val = sem.bump(16)
        def g(e, out_ap=out_ap, in_ap=in_ap, waits=tuple(waits)):
            for (s, v) in waits:
                if v > 0:
                    e.wait_ge(s.h, v)
            e.dma_start(out=out_ap, in_=in_ap).then_inc(sem.h, 16)
        ops[eng].append(g)
        return val

    dma("sp", CST, idf[:], idf_h)
    dma("sp", CST, gamt[:], gam_h)
    w_cst = dma("sp", CST, clip_t[:], clip_h)

    em("dv", [], lambda e: e.memset(ones_r[:], 1.0).then_inc(DV.h, 1))
    v_ones = DV.bump()

    em("pe", [(CST, w_cst)],
       lambda e: e.transpose(pb[6][0:128, 0:8], gamt[:, :], idf[0:8, 0:8]).then_inc(PE.h, 1))
    v_gt = PE.bump()
    em("dv", [(PE, v_gt)],
       lambda e: e.tensor_copy(gamc[:], pb[6][:, 0:8]).then_inc(DV.h, 1))
    v_gc = DV.bump()

    xld1 = {}
    bn_dv = {}
    v_row3b = {}

    def emit_rowmath(b):
        sl = slice(b * 8, b * 8 + 8)

        def f(e, sl=sl):
            e.tensor_copy(muc[:, sl],
                          aggr[:].rearrange("p (t k) -> p t k", k=2)[:, sl, 0])
            e.tensor_copy(varc[:, sl],
                          aggr[:].rearrange("p (t k) -> p t k", k=2)[:, sl, 1])
            e.drain()
            e.tensor_scalar_add(rtmp[:, sl], varc[:, sl], 1e-5)
            e.drain()
            e.reciprocal(rc[:, sl], rtmp[:, sl])
            e.drain().then_inc(DV.h, 1)
        em("dv", [(DV, bn_dv[b * 8 + 7])], f)
        v_row1 = DV.bump()

        em("ac", [(DV, v_row1)],
           lambda e, sl=sl: e.activation(rc[:, sl], rc[:, sl],
                                         AF.Sqrt).then_inc(AC.h, 1))
        v_rsq = AC.bump()

        def f(e, sl=sl):
            e.tensor_mul(rtmp[:, sl], rc[:, sl], rc[:, sl])
            e.drain()
            e.tensor_mul(rtmp[:, sl], rtmp[:, sl], varc[:, sl])
            e.drain()
            e.tensor_scalar_mul(rtmp[:, sl], rtmp[:, sl], 1024.0)
            e.drain().then_inc(DV.h, 1)
        em("dv", [(AC, v_rsq)], f)
        v_row2 = DV.bump()

        em("ac", [(DV, v_row2)],
           lambda e, sl=sl: e.activation(rtmp[:, sl], rtmp[:, sl],
                                         AF.Sqrt).then_inc(AC.h, 1))
        v_rsq2 = AC.bump()

        def f(e, sl=sl):
            e.tensor_scalar_max(rtmp[:, sl], rtmp[:, sl], 1e-12)
            e.drain()
            e.reciprocal(rtmp[:, sl], rtmp[:, sl])
            e.drain()
            e.tensor_scalar_mul(rtmp[:, sl], rtmp[:, sl], 32.0)
            e.drain()
            e.tensor_mul(hscc[:, sl], rc[:, sl], rtmp[:, sl])
            e.drain().then_inc(DV.h, 1)
        em("dv", [(AC, v_rsq2)], f)
        v_row3b[b] = DV.bump()

    def emit_a1(i):
        slot = i % 2
        xld1[i] = (A1L[slot], dma("gp", A1L[slot], a1buf[:, slot, :],
                                 x_h[i * 128:(i + 1) * 128, :],
                                 [(DV, bn_dv.get(i - 2, 0))]))

        def f(e, i=i, slot=slot):
            e.bn_stats(bnst[:, i * 12:i * 12 + 6], a1buf[:, slot, 0:512])
            e.bn_stats(bnst[:, i * 12 + 6:i * 12 + 12],
                       a1buf[:, slot, 512:1024])
            e.drain()
            e.bn_aggr(aggr[:, i * 2:i * 2 + 2],
                      bnst[:, i * 12:i * 12 + 12]).then_inc(DV.h, 1)
        em("dv", [xld1[i]], f)
        bn_dv[i] = DV.bump()
        if i < MCQ:
            emit_wq(i)
        elif i < MCQ + HCN:
            emit_down(i - MCQ)
        if i % 8 == 7:
            emit_rowmath(i // 8)

    fq_ac = {}
    fq_st = {}

    def emit_down(ch):
        slot = ch % 2
        LSEM = FW[slot]
        waits = []
        if ch >= 2:
            waits.append((AC, fq_ac[("d", ch - 2)]))
        ld_v = (LSEM, dma("ac", LSEM, wdstg[:, slot, :],
                          wd_h[ch * 128:(ch + 1) * 128, :], waits))

        def f(e, ld_v=ld_v, slot=slot, ch=ch):
            e.tensor_reduce(swd[:, ch:ch + 1], wdstg[:, slot, :], AX.X,
                            ALU.max, apply_absolute_value=True)
            e.drain()
            e.reciprocal(invd[:, ch:ch + 1], swd[:, ch:ch + 1])
            e.drain()
            e.tensor_scalar_mul(invd[:, ch:ch + 1], invd[:, ch:ch + 1],
                                127.0).then_inc(DV.h, 1)
        em("dv", [ld_v], f)
        v_sc = DV.bump()

        st_prev = fq_st[("d", ch - 2)] if ch >= 2 else (FQW[0], 0)

        def f(e, v_sc=v_sc, slot=slot, ch=ch):
            e.activation(wd8b_s[slot], wdstg[:, slot, :], AF.Copy,
                         scale=invd[:, ch:ch + 1]).then_inc(AC.h, 1)
        em("ac", [(DV, v_sc), st_prev], f)
        fq_ac[("d", ch)] = AC.bump()
        fq_st[("d", ch)] = (FQW[slot], dma(
            "sp", FQW[slot], d8_h[ch * 128:(ch + 1) * 128, :], wd8b_s[slot],
            [(AC, fq_ac[("d", ch)])]))

    gu_chunks = []
    for ch in range(ICN):
        gu_chunks.append(("g", wg_h, g8_h, swg, invg, ch))
        gu_chunks.append(("u", wu_h, u8_h, swu, invu, ch))

    def emit_gu_chunk(gk):
        name, src_h, dst_h, sw, inv, ch = gu_chunks[gk]
        slot = gk % 2
        LSEM = FN[slot]
        waits = [(AC, v_fqd_ac)]
        if gk >= 2:
            pn, pch = gu_chunks[gk - 2][0], gu_chunks[gk - 2][5]
            waits.append((AC, fq_ac[(pn, pch)]))

        ld_v = (LSEM, dma("ac", LSEM, wstg[:, slot, :],
                          src_h[ch * 128:(ch + 1) * 128, :], waits))

        def f(e, ld_v=ld_v, slot=slot, sw=sw, inv=inv, ch=ch):
            e.tensor_reduce(sw[:, ch:ch + 1], wstg[:, slot, :], AX.X,
                            ALU.max, apply_absolute_value=True)
            e.drain()
            e.reciprocal(inv[:, ch:ch + 1], sw[:, ch:ch + 1])
            e.drain()
            e.tensor_scalar_mul(inv[:, ch:ch + 1], inv[:, ch:ch + 1],
                                127.0).then_inc(DV.h, 1)
        em("dv", [ld_v], f)
        v_sc = DV.bump()

        if gk >= 2:
            pn, pch = gu_chunks[gk - 2][0], gu_chunks[gk - 2][5]
            st_prev = fq_st[(pn, pch)]
        else:
            st_prev = (FQN[0], 0)

        def f(e, v_sc=v_sc, slot=slot, inv=inv, ch=ch):
            e.activation(w8stg[:, slot, :], wstg[:, slot, :], AF.Copy,
                         scale=inv[:, ch:ch + 1]).then_inc(AC.h, 1)
        em("ac", [(DV, v_sc), st_prev], f)
        fq_ac[(name, ch)] = AC.bump()
        fq_st[(name, ch)] = (FQN[slot], dma(
            "sp", FQN[slot], dst_h[ch * 128:(ch + 1) * 128, :],
            w8stg[:, slot, :], [(AC, fq_ac[(name, ch)])]))

    wq_ld = {}
    wq_pe = {}
    wq_ac = {}
    wq_dv = {}

    wqs4 = hilo[:].rearrange("p a k n -> p (a k n)").bitcast(F32).rearrange(
        "p (s n) -> p s n", s=4)
    NWQ = MCQ

    def emit_wq(j):
        kc, t = j // 3, j % 3
        slot = j % 4
        waits = [(AC, wq_ac.get(j - 4, 0)), (DV, wq_dv.get(j - 4, 0)),
                 (AC, wq_ac.get(j - 2, 0))]
        wq_ld[j] = (WQ[slot % 2], dma(
            "sp", WQ[slot % 2], wqs4[:, slot, :],
            wq_h[kc * 128:(kc + 1) * 128, t * 1024:(t + 1) * 1024], waits))

        def f(e, j=j, kc=kc, t=t, slot=slot):
            e.activation(whi[:, kc, t * 8:(t + 1) * 8, :], wqs4[:, slot, :],
                         AF.Copy, scale=gamc[:, kc:kc + 1]).then_inc(AC.h, 1)
        em("ac", [wq_ld[j], (DV, v_gc)], f)
        wq_ac[j] = AC.bump()

        def f(e, j=j, kc=kc, t=t, slot=slot):
            e.scalar_tensor_tensor(
                wlo[:, kc, t * 8:(t + 1) * 8, :], wqs4[:, slot, :],
                gamc[:, kc:kc + 1], whi[:, kc, t * 8:(t + 1) * 8, :],
                ALU.mult, ALU.subtract).then_inc(DV.h, 1)
        em("dv", [(PE, 0), (AC, wq_ac[j])], f)
        wq_dv[j] = DV.bump()


    for i in range(NT):
        emit_a1(i)
    fqw_totals = [(FQW[0], FQW[0].v), (FQW[1], FQW[1].v)]
    v_fqd_dv = DV.v
    v_fqd_ac = AC.v

    gu_ptr = [0]
    xld = {}
    xn_dv = {}
    xn_st = {}
    hq_dv = {}
    tr_pe = {}
    hl_ac = {}
    hl_dv = {}
    mm_pe = {}
    ta_ac = {}
    tq_dv = {}
    qy_st = {}

    def emit_sub_prep(i):
        tt, sub = i // 4, i % 4
        hl = hilo if tt % 2 == 0 else hilo2
        if True:
            slot = i % 2
            xld[i] = (XL[slot], dma("gp", XL[slot], xbuf[:, slot, :],
                                    x_h[i * 128:(i + 1) * 128, :],
                                    [(DV, v_row3b[i // 8]), (DV, hq_dv.get(i - 2, 0))]))

            def f(e, i=i, slot=slot):
                e.tensor_scalar(xnb[:, slot, :], xbuf[:, slot, :],
                                muc[:, i:i + 1], rc[:, i:i + 1],
                                ALU.subtract, ALU.mult).then_inc(DV.h, 1)
                e.drain()
                e.tensor_reduce(xmaxc[:, i:i + 1], xnb[:, slot, :], AX.X,
                                ALU.max, apply_absolute_value=True).then_inc(DV.h, 1)
            em("dv", [xld[i], (DV, v_row3b[i // 8]),
                      xn_st.get(i - 2, (XNS[0], 0))], f)
            xn_dv[i] = DV.bump(2)
            xn_st[i] = (XNS[slot], dma("sp", XNS[slot],
                                       xn_h[i * 128:(i + 1) * 128, :],
                                       xnb[:, slot, :], [(DV, xn_dv[i])]))

            def f(e, i=i, slot=slot):
                e.tensor_scalar(hqb[:, slot, :], xbuf[:, slot, :],
                                muc[:, i:i + 1], hscc[:, i:i + 1],
                                ALU.subtract, ALU.mult).then_inc(DV.h, 1)
            em("dv", [(PE, tr_pe.get(i - 2, 0))], f)
            hq_dv[i] = DV.bump()

    def emit_sub_tr(i):
        tt, sub = i // 4, i % 4
        hl = hilo if tt % 2 == 0 else hilo2
        slot = i % 2
        if True:
            def f(e, i=i, slot=slot):
                for kc in range(KC):
                    bank = pb[6] if kc < 4 else pb[7]
                    ins = e.transpose(bank[:, (kc % 4) * 128:(kc % 4 + 1) * 128],
                                      hqb[:, slot, kc * 128:(kc + 1) * 128],
                                      idf[:])
                    if kc in (3, 7):
                        ins.then_inc(PE.h, 1)
            em("pe", [(DV, hq_dv[i]), (AC, hl_ac.get(i - 1, 0)),
                      (DV, hl_dv.get(i - 1, 0))], f)
            tr_pe[i] = PE.bump(2)

            w_extra = (PE, mm_pe.get((tt - 2, 7), 0)) if sub == 0 else (PE, 0)
            w_wq = ((DV, wq_dv[NWQ - 1]), (AC, wq_ac[NWQ - 1])) \
                if (tt == 0 and sub == 0) else ((DV, 0), (AC, 0))
            w_dprep = ((DV, v_fqd_dv) if (tt == 1 and sub == 0) else (DV, 0),
                       (AC, v_fqd_ac) if (tt == 1 and sub == 0) else (AC, 0))


            def f(e, i=i, sub=sub, hl=hl):
                for half in range(2):
                    bank = pb[6 + half]
                    dst = hl[:, 0, half * 4:half * 4 + 4,
                             sub * 128:(sub + 1) * 128]
                    e.activation(dst, bank[:, 0:512],
                                 AF.Copy).then_inc(AC.h, 1)
            em("ac", [(PE, tr_pe[i]), w_extra, w_dprep[1], w_wq[1]], f)
            hl_ac[i] = AC.bump(2)

            def f(e, i=i, sub=sub, hl=hl):
                for half in range(2):
                    bank = pb[6 + half]
                    dst = hl[:, 1, half * 4:half * 4 + 4,
                             sub * 128:(sub + 1) * 128]
                    src_hi = hl[:, 0, half * 4:half * 4 + 4,
                                sub * 128:(sub + 1) * 128]
                    e.tensor_tensor(dst, bank[:, 0:512], src_hi,
                                    ALU.subtract).then_inc(DV.h, 1)
            em("dv", [(PE, tr_pe[i]), (AC, hl_ac[i]), w_extra,
                      w_dprep[0], w_wq[0]], f)
            hl_dv[i] = DV.bump(2)

    def emit_group(tt, g):
        hl = hilo if tt % 2 == 0 else hilo2
        if True:
            ps = g % 2
            banks = (pb[3 * ps], pb[3 * ps + 1], pb[3 * ps + 2])
            prev = tq_dv.get((tt, g - 2), 0) or (
                tq_dv.get((tt - 1, g + 6), 0) if g < 2 else 0)
            preva = ta_ac.get((tt, g - 2), 0) or (
                ta_ac.get((tt - 1, g + 6), 0) if g < 2 else 0)

            def f(e, tt=tt, g=g, banks=banks, hl=hl):
                for bi, mc in enumerate((g, 8 + g, 16 + g)):
                    cnt = 0
                    for wt, xi in ((whi, 0), (whi, 1), (wlo, 0)):
                        for kc in range(KC):
                            ins = e.matmul(banks[bi][:, :], wt[:, kc, mc, :],
                                           hl[:, xi, kc, :],
                                           start=(cnt == 0), stop=(cnt == 23))
                            cnt += 1
                    ins.then_inc(PE.h, 1)
            em("pe", [(AC, hl_ac[tt * 4 + 3]), (DV, hl_dv[tt * 4 + 3]),
                      (DV, prev), (AC, preva)], f)
            mm_pe[(tt, g)] = PE.bump(3)

            gs = g % 2

            def f(e, tt=tt, g=g, banks=banks, gs=gs):
                e.activation(af[:, gs, :], banks[2][:, :],
                             AF.Sigmoid).then_inc(AC.h, 1)
            em("ac", [(PE, mm_pe[(tt, g)]), (DV, prev)], f)
            ta_ac[(tt, g)] = AC.bump()

            def f(e, tt=tt, g=g, banks=banks, gs=gs):
                init = 0.0 if tt == 0 else carry[:, g:g + 1]
                e.tensor_tensor_scan(yf[:, gs, :], af[:, gs, :],
                                     banks[1][:, :], init, ALU.mult, ALU.add)
                e.drain()
                e.tensor_copy(carry[:, g:g + 1], yf[:, gs, 511:512])
                e.tensor_tensor(qyb[:, gs, :], banks[0][:, :], yf[:, gs, :],
                                ALU.mult)
                e.drain().then_inc(DV.h, 1)
            em("dv", [(PE, mm_pe[(tt, g)]), (AC, ta_ac[(tt, g)]),
                      qy_st.get((tt, g - 2), (QYS[0], 0)) if g >= 2 else
                      qy_st.get((tt - 1, g + 6), (QYS[0], 0))], f)
            tq_dv[(tt, g)] = DV.bump()

            qy_st[(tt, g)] = (QYS[g], dma(
                "sp", QYS[g],
                qy_h[g * 128:(g + 1) * 128, tt * 512:(tt + 1) * 512],
                qyb[:, gs, :], [(DV, tq_dv[(tt, g)])]))

    for sub in range(4):
        emit_sub_prep(sub)
        emit_sub_tr(sub)
    for tt in range(NTT):
        for g in range(4):
            emit_group(tt, g)
        if tt + 1 < NTT:
            b = (tt + 1) * 4
            emit_sub_prep(b)
            emit_sub_prep(b + 1)
            emit_sub_tr(b)
        for g in range(4, 8):
            emit_group(tt, g)
            if tt + 1 < NTT:
                b = (tt + 1) * 4
                if g == 4:
                    emit_sub_prep(b + 2)
                    emit_sub_tr(b + 1)
                elif g == 5:
                    emit_sub_prep(b + 3)
                    emit_sub_tr(b + 2)
                elif g == 6:
                    emit_sub_tr(b + 3)
        while gu_ptr[0] < min(len(gu_chunks), (tt + 1) * 7):
            emit_gu_chunk(gu_ptr[0])
            gu_ptr[0] += 1

    vA = {"pe": PE.v, "dv": DV.v, "ac": AC.v}
    qys_totals = [(QYS[g], QYS[g].v) for g in range(8)]
    fq_totals = fqw_totals + [(FQN[0], FQN[0].v), (FQN[1], FQN[1].v)]
    v_fq_dv = DV.v

    def f(e):
        e.tensor_reduce(vmaxr[:], xmaxc[:], AX.X, ALU.max)
        e.drain().then_inc(DV.h, 1)
    em("dv", [(DV, xn_dv[NT - 1])], f)
    v_xm1 = DV.bump()

    em("pe", [(DV, v_xm1), (PE, vA["pe"])],
       lambda e: e.transpose(pb[6][0:1, 0:128], vmaxr[:, 0:1],
                             idf[:]).then_inc(PE.h, 1))
    v_xm2 = PE.bump()

    def f(e):
        e.memset(smt[:], 0.0)
        e.drain()
        e.tensor_reduce(smt[:, 0:1], pb[6][0:1, 0:128], AX.X, ALU.max)
        e.drain().then_inc(DV.h, 1)
    em("dv", [(PE, v_xm2)], f)
    v_xm3 = DV.bump()

    cc1w = dma("gp", CC, cc1i.ap(), smt[:], [(DV, v_xm3)])

    def f(e):
        e.collective_compute(
            "AllReduce", ALU.max, replica_groups=[list(range(8))],
            ins=[cc1i.ap().opt()], outs=[cc1o.ap().opt()],
        ).then_inc(CCE.h, 1)
    em("gp", [(CC, cc1w)], f)
    v_cc1 = CCE.bump()

    gu_ld = {}
    gu_cast = {}
    gu_pe = {}
    gu_dv = {}
    order = [("g", g8_h, qwg), ("u", u8_h, qwu)]
    for oi, (name, src8, dstw) in enumerate(order):
        for ic in range(ICN):
            k = oi * ICN + ic
            slot = k % 2
            gu_ld[k] = (GUL[slot], dma(
                "gp", GUL[slot], w8stg[:, slot, :],
                src8[ic * 128:(ic + 1) * 128, :],
                fq_totals + [(DV, gu_cast.get(k - 2, 0))]))

            def f(e, k=k, slot=slot):
                e.tensor_copy(wbstg[:, slot, :], w8stg[:, slot, :])
                e.drain().then_inc(DV.h, 1)
            em("dv", [gu_ld[k],
                      (PE, mm_pe[(NTT - 1, 7)]),
                      (DV, gu_dv.get(k - 2, 0))], f)
            gu_cast[k] = DV.bump()

            def f(e, k=k, slot=slot):
                for kc in range(KC):
                    bank = pb[6] if kc < 4 else pb[7]
                    ins = e.transpose(bank[:, (kc % 4) * 128:(kc % 4 + 1) * 128],
                                      wbstg[:, slot, kc * 128:(kc + 1) * 128],
                                      idf[:])
                    if kc in (3, 7):
                        ins.then_inc(PE.h, 1)
            em("pe", [(DV, gu_cast[k]), (PE, vA["pe"]),
                      (DV, gu_dv.get(k - 1, 0)), (DV, v_xm3 if k == 0 else 0)], f)
            gu_pe[k] = PE.bump(2)

            def f(e, k=k, name=name, ic=ic, dstw=dstw):
                for half in range(2):
                    bank = pb[6 + half]
                    dst = dstw[:, half * 4:half * 4 + 4, ic, :]
                    e.tensor_copy(dst, bank[:, 0:512]).then_inc(DV.h, 1)
            em("dv", [(PE, gu_pe[k])], f)
            gu_dv[k] = DV.bump(2)
    v_gu_dv = DV.v

    cc1r = dma("gp", CC, smt2[:], cc1o.ap(), [(CCE, v_cc1)])

    em("ac", [(CST, w_cst)],
       lambda e: e.activation(s1s2[:, 2:4], clip_t[:, 0:2],
                              AF.Sigmoid).then_inc(AC.h, 1))
    v_sig = AC.bump()

    def f(e):
        e.tensor_mul(smt2[:, 2:3], smt2[:, 0:1], s1s2[:, 2:3])
        e.drain()
        e.tensor_scalar_mul(smt2[:, 2:3], smt2[:, 2:3], QMAXI)
        e.drain()
        e.reciprocal(smt2[:, 1:2], smt2[:, 2:3])
        e.drain()
        e.tensor_scalar_mul(smt2[:, 2:3], smt2[:, 2:3], QMAXI)
        e.drain().then_inc(DV.h, 1)
    em("dv", [(CC, cc1r), (AC, v_sig)], f)
    v_s1 = DV.bump()

    em("pe", [(DV, v_s1), (DV, v_ones), (PE, vA["pe"])],
       lambda e: e.matmul(pb[0][:, 0:2], ones_r[:, :], smt2[0:1, 1:3],
                          start=True, stop=True).then_inc(PE.h, 1))
    v_bc1 = PE.bump()

    def f(e):
        e.tensor_copy(inv1c[:], pb[0][:, 0:1])
        e.tensor_copy(s1col[:], pb[0][:, 1:2])
        e.drain()
        e.tensor_scalar(sgate[:], swg[:], s1col[:, 0:1], None,
                        ALU.mult)
        e.drain()
        e.tensor_mul(comb[:], sgate[:], swu[:])
        e.drain()
        e.tensor_scalar(comb[:], comb[:], s1col[:, 0:1], None, ALU.mult)
        e.drain().then_inc(DV.h, 1)
    em("dv", [(PE, v_bc1), (DV, v_fq_dv)], f)
    v_bc1c = DV.bump()

    xnld = {}
    xq_ac = {}
    xq_dv = {}
    xqt_pe = {}
    xqe_dv = {}
    gmm_pe = {}
    sg_ac = {}
    vv_dv = {}
    vv_gp = {}
    vmx_dv = {}
    vst = {}
    upf = qf
    bu_of = {ic: pb[2 + (ic % 2)] for ic in range(ICN)}

    def emit_xqp_prep(i):
        tt, sub = i // 4, i % 4
        xqf = xqfm if tt % 2 == 0 else xqfm2
        if True:
            slot = i % 2
            xnld[i] = (XL[slot], dma(
                "gp", XL[slot], xbuf[:, slot, :],
                xn_h[i * 128:(i + 1) * 128, :],
                [xn_st[i], (DV, v_bc1c), (AC, xq_ac.get(i - 2, 0))]))

            def f(e, i=i, slot=slot):
                e.activation(xq8[:, slot, :], xbuf[:, slot, :], AF.Copy,
                             scale=inv1c[:, 0:1]).then_inc(AC.h, 1)
            em("ac", [xnld[i], (DV, v_bc1c),
                      (DV, xq_dv.get(i - 2, 0)), (AC, vA["ac"])], f)
            xq_ac[i] = AC.bump()

            def f(e, i=i, slot=slot):
                e.tensor_copy(xnb[:, slot, :],
                              xq8[:, slot, :]).then_inc(DV.h, 1)
            em("dv", [(AC, xq_ac[i]),
                      (PE, xqt_pe.get(i - 2, 0)),
                      xn_st[NT - 1] if i < 2 else (XNS[0], 0),
                      xn_st[NT - 2] if i < 2 else (XNS[1], 0)], f)
            xq_dv[i] = DV.bump()

    def emit_xqp_tr(i):
        tt, sub = i // 4, i % 4
        xqf = xqfm if tt % 2 == 0 else xqfm2
        slot = i % 2
        if True:
            xbase = 4 if i % 2 == 0 else 6

            def f(e, i=i, slot=slot, xbase=xbase):
                for kc in range(KC):
                    bank = pb[xbase] if kc < 4 else pb[xbase + 1]
                    ins = e.transpose(bank[:, (kc % 4) * 128:(kc % 4 + 1) * 128],
                                      xnb[:, slot, kc * 128:(kc + 1) * 128],
                                      idf[:])
                    if kc in (3, 7):
                        ins.then_inc(PE.h, 1)
            em("pe", [(DV, xq_dv[i]),
                      (DV, xqe_dv.get(i - 2, 0)),
                      (DV, vA["dv"] if i == 1 else 0),
                      (AC, vA["ac"] if i == 1 else 0)], f)
            xqt_pe[i] = PE.bump(2)

            wx = (PE, gmm_pe.get((tt - 2, ICN - 1), 0)) if sub == 0 else (PE, 0)

            def f(e, i=i, sub=sub, xqf=xqf, xbase=xbase):
                for half in range(2):
                    bank = pb[xbase + half]
                    dst = xqf[:, half * 4:half * 4 + 4,
                              sub * 128:(sub + 1) * 128]
                    e.tensor_copy(dst, bank[:, 0:512]).then_inc(DV.h, 1)
            em("dv", [(PE, xqt_pe[i]), wx], f)
            xqe_dv[i] = DV.bump(2)

    def emit_ic(tt, ic):
        xqf = xqfm if tt % 2 == 0 else xqfm2
        if True:
            gs = ic % 2
            bg, bu = pb[gs], pb[2 + gs]
            prev = vv_dv.get((tt, ic - 2), 0) or (
                vv_dv.get((tt - 1, ICN - 2 + ic), 0) if ic < 2 else 0)

            def f(e, tt=tt, ic=ic, bg=bg, bu=bu, xqf=xqf):
                for kc in range(KC):
                    ins = e.matmul(bg[:, :], qwg[:, kc, ic, :], xqf[:, kc, :],
                                   start=kc == 0, stop=kc == KC - 1)
                ins.then_inc(PE.h, 1)
                for kc in range(KC):
                    ins = e.matmul(bu[:, :], qwu[:, kc, ic, :],
                                   xqf[:, kc, :],
                                   start=kc == 0, stop=kc == KC - 1)
                ins.then_inc(PE.h, 1)
            em("pe", [(DV, xqe_dv[tt * 4 + 3]), (DV, prev),
                      (AC, sg_ac.get((tt, ic - 2), 0)),
                      (DV, gu_dv[2 * ic + 1] if tt == 0 else 0),
                      (DV, v_bc1c if (tt == 0 and ic == 0) else 0)], f)
            gmm_pe[(tt, ic)] = PE.bump(2)

            def f(e, tt=tt, ic=ic, bg=bg, gs=gs):
                e.activation(sigb[:, gs, :], bg[:, :], AF.Sigmoid,
                             scale=sgate[:, ic:ic + 1]).then_inc(AC.h, 1)
            em("ac", [(PE, gmm_pe[(tt, ic)]), (DV, prev),
                      (DV, vA["dv"] if (tt == 0 and ic < 2) else 0)], f)
            sg_ac[(tt, ic)] = AC.bump()

            half = ic // 11
            icc = ic % 11
            if icc == 0:
                if tt == 0 and half == 0:
                    w_vfree = (AC, v_fqd_ac)
                else:
                    prev_half = (tt, 0) if half == 1 else (tt - 1, 1)
                    w_vfree = (VS, vst[prev_half])
            else:
                w_vfree = (VS, 0)

            stt_v = DV.v + 1

            def f(e, tt=tt, ic=ic, bg=bg, bu=bu, gs=gs, icc=icc, stt_v=stt_v):
                e.scalar_tensor_tensor(vtmp[:, gs, :], sigb[:, gs, :],
                                       comb[:, ic:ic + 1], bg[:, :],
                                       ALU.mult, ALU.mult).then_inc(DV.h, 1)
                e.wait_ge(DV.h, stt_v)
                e.tensor_tensor(vstage[:, icc, :], vtmp[:, gs, :],
                                bu[:, :], ALU.mult).then_inc(DV.h, 1)
            em("dv", [(PE, gmm_pe[(tt, ic)]), (AC, sg_ac[(tt, ic)]),
                      w_vfree,
                      (DV, vmx_dv.get((tt, half - 1) if half else
                                      (tt - 1, 1), 0))], f)
            DV.bump()
            vv_dv[(tt, ic)] = DV.bump()
            vv_gp[(tt, ic)] = 0

            if icc == 10:
                def f(e, tt=tt, half=half):
                    e.tensor_reduce(vmaxc[:, half:half + 1],
                                    vstage[:].rearrange("p a b -> p (a b)"),
                                    AX.X, ALU.max, apply_absolute_value=True)
                    e.drain()
                    if tt == 0 and half == 0:
                        e.tensor_copy(vmaxrun[:, 0:1], vmaxc[:, 0:1])
                    else:
                        e.tensor_tensor(vmaxrun[:, 0:1], vmaxrun[:, 0:1],
                                        vmaxc[:, half:half + 1], ALU.max)
                    e.drain().then_inc(DV.h, 1)
                em("dv", [(DV, vv_dv[(tt, ic)])], f)
                vmx_dv[(tt, half)] = DV.bump()

                vst[(tt, half)] = dma(
                    "sp", VS,
                    v_h.rearrange("(ic p) n -> p ic n", p=128)
                    [:, half * 11:(half + 1) * 11, tt * 512:(tt + 1) * 512],
                    vstage[:, 0:11, :], [(DV, vv_dv[(tt, ic)])])

    for sub in range(4):
        emit_xqp_prep(sub)
        emit_xqp_tr(sub)
    for tt in range(NTT):
        for ic in range(11):
            emit_ic(tt, ic)
        if tt + 1 < NTT:
            b = (tt + 1) * 4
            emit_xqp_prep(b)
            emit_xqp_prep(b + 1)
            emit_xqp_tr(b)
        for ic in range(11, ICN):
            emit_ic(tt, ic)
            if tt + 1 < NTT:
                b = (tt + 1) * 4
                if ic == 11:
                    emit_xqp_prep(b + 2)
                    emit_xqp_tr(b + 1)
                elif ic == 12:
                    emit_xqp_prep(b + 3)
                    emit_xqp_tr(b + 2)
                elif ic == 13:
                    emit_xqp_tr(b + 3)

    vB = {"pe": PE.v, "dv": DV.v, "ac": AC.v, "gp": GP.v}

    def f(e):
        e.tensor_copy(vmaxr[:], vmaxrun[:, 0:1])
        e.drain().then_inc(DV.h, 1)
    em("dv", [(DV, vmx_dv[(NTT - 1, 1)])], f)
    v_vm1 = DV.bump()

    em("pe", [(DV, v_vm1), (PE, vB["pe"]), (DV, vB["dv"])],
       lambda e: e.transpose(pb[6][0:1, 0:128], vmaxr[:, 0:1],
                             idf[:]).then_inc(PE.h, 1))
    v_vm2 = PE.bump()

    def f(e):
        e.memset(smt[:], 0.0)
        e.drain()
        e.tensor_reduce(smt[:, 0:1], pb[6][0:1, 0:128], AX.X, ALU.max)
        e.drain().then_inc(DV.h, 1)
    em("dv", [(PE, v_vm2)], f)
    v_vm3 = DV.bump()

    cc2w = dma("gp", CC, cc2i.ap(), smt[:], [(DV, v_vm3)])

    def f(e):
        e.collective_compute(
            "AllReduce", ALU.max, replica_groups=[list(range(8))],
            ins=[cc2i.ap().opt()], outs=[cc2o.ap().opt()],
        ).then_inc(CCE.h, 1)
    em("gp", [(CC, cc2w)], f)
    v_cc2 = CCE.bump()

    d_ld = {}
    d_cast = {}
    d_pe = {}
    d_dv = {}
    NGRP = 6
    for hc in range(HCN):
        slot = hc % 2
        d_ld[hc] = (DL[slot], dma(
            "gp", DL[slot], wd8b_s[slot],
            d8_h[hc * 128:(hc + 1) * 128, :],
            [(PE, vB["pe"]), (DV, vB["dv"]), (AC, vB["ac"]),
             (GP, vB["gp"]), (DV, d_cast.get(hc - 2, 0))]))

        def f(e, hc=hc, slot=slot):
            e.tensor_copy(wdstg[:, slot, :], wd8b_s[slot])
            e.drain().then_inc(DV.h, 1)
        em("dv", [d_ld[hc], (VS, vst[(NTT - 1, 1)]),
                  (DV, d_dv.get((hc - 2, NGRP - 1), 0)),
                  (DV, v_vm3 if hc == 0 else 0)], f)
        d_cast[hc] = DV.bump()

        for grp in range(NGRP):
            lo_ic = grp * 4
            hi_ic = min(lo_ic + 4, ICN)

            def f(e, hc=hc, slot=slot, lo_ic=lo_ic, hi_ic=hi_ic, grp=grp):
                bank = pb[6 + grp % 2]
                for j, icb in enumerate(range(lo_ic, hi_ic)):
                    ins = e.transpose(bank[:, j * 128:(j + 1) * 128],
                                      wdstg[:, slot,
                                            icb * 128:(icb + 1) * 128],
                                      idf[:])
                ins.then_inc(PE.h, 1)
            em("pe", [(DV, d_cast[hc]), (PE, vB["pe"]),
                      (DV, d_dv.get((hc, grp - 2), 0) or
                       (d_dv.get((hc - 1, NGRP - 2 + grp), 0)
                        if grp < 2 else 0)),
                      (DV, v_vm3 if (hc == 0 and grp < 2) else 0)], f)
            d_pe[(hc, grp)] = PE.bump()

            def f(e, hc=hc, grp=grp, lo_ic=lo_ic, hi_ic=hi_ic):
                bank = pb[6 + grp % 2]
                e.tensor_copy(qwd[:, lo_ic:hi_ic, hc, :],
                              bank[:, 0:(hi_ic - lo_ic) * 128]).then_inc(DV.h, 1)
            em("dv", [(PE, d_pe[(hc, grp)])], f)
            d_dv[(hc, grp)] = DV.bump()
    v_d_dv = DV.v

    cc2r = dma("gp", CC, smt2[:, 4:6], cc2o.ap()[0:1, 0:2], [(CCE, v_cc2)])

    def f(e):
        e.tensor_mul(smt2[:, 5:6], smt2[:, 4:5], s1s2[:, 3:4])
        e.drain()
        e.tensor_scalar_mul(smt2[:, 5:6], smt2[:, 5:6], QMAXI)
        e.drain()
        e.reciprocal(smt2[:, 6:7], smt2[:, 5:6])
        e.drain()
        e.tensor_scalar_mul(smt2[:, 5:6], smt2[:, 5:6], QMAXI)
        e.drain().then_inc(DV.h, 1)
    em("dv", [(CC, cc2r), (AC, v_sig)], f)
    v_s2 = DV.bump()

    em("pe", [(DV, v_s2), (PE, vB["pe"])],
       lambda e: e.matmul(pb[0][:, 0:2], ones_r[:, :], smt2[0:1, 5:7],
                          start=True, stop=True).then_inc(PE.h, 1))
    v_bc2 = PE.bump()

    def f(e):
        e.tensor_copy(s2col[:], pb[0][:, 0:1])
        e.tensor_copy(inv2c[:], pb[0][:, 1:2])
        e.drain()
        e.tensor_scalar(sdown[:], swd[:], s2col[:, 0:1], None, ALU.mult)
        e.drain().then_inc(DV.h, 1)
    em("dv", [(PE, v_bc2)], f)
    v_bc2c = DV.bump()

    vld = {}
    h2_ac = {}
    h2_dv = {}
    dmm_pe = {}
    ds_dv = {}
    otr_pe = {}
    oev_dv = {}
    out_st = {}
    qyl = {}
    xld3 = {}

    qyv = {}
    for hc in range(4):
        qyv[hc] = xnb[:].rearrange("p a b -> p (a b)").rearrange(
            "p (h n) -> p h n", h=4)[:, hc, :]
    qyv[4] = af[:, 0, :]
    qyv[5] = af[:, 1, :]
    qyv[6] = yf[:, 0, :]
    qyv[7] = yf[:, 1, :]
    qyld4 = xnb[:].rearrange("p a b -> p (a b)").rearrange(
        "p (h n) -> p h n", h=4)

    h2q_ac = {}
    h2c_dv = {}

    def emit_cquant(tt, half):
        h2b = h2b2[tt % 2]
        if True:
            prev_h = (tt, 0) if half == 1 else (tt - 1, 1)
            for qtr, (qlo, qhi) in enumerate(((0, 6), (6, 11))):
                w_free = [(VS, vst[(tt, half)]), (DV, v_d_dv)]
                qk = (prev_h[0], prev_h[1], 5 if qtr == 0 else 10)
                if qk in h2q_ac:
                    w_free.append((AC, h2q_ac[qk]))
                vld[(tt, half, qtr)] = (VL[qtr], dma(
                    "gp", VL[qtr], vstage[:, qlo:qhi, :],
                    v_h.rearrange("(ic p) n -> p ic n", p=128)
                    [:, half * 11 + qlo:half * 11 + qhi,
                     tt * 512:(tt + 1) * 512],
                    w_free))

            for icc in range(11):
                ic = half * 11 + icc

                def f(e, tt=tt, half=half, icc=icc, ic=ic):
                    qs = icc % 2
                    off = (icc % 4 // 2) * 512
                    e.activation(h28[:, qs, off:off + 512], vstage[:, icc, :],
                                 AF.Copy,
                                 scale=inv2c[:, 0:1]).then_inc(AC.h, 1)
                em("ac", [vld[(tt, half, 0 if icc < 6 else 1)], (DV, v_bc2c),
                          (AC, vB["ac"] if (tt == 0 and half == 0) else 0),
                          (DV, h2c_dv.get((tt, half, icc - 4), 0) or
                           (h2c_dv.get(prev_h + (10 - ((10 - icc) % 4),), 0)
                            if icc < 4 else 0))], f)
                h2q_ac[(tt, half, icc)] = AC.bump()

                def f(e, tt=tt, half=half, icc=icc, ic=ic, h2b=h2b):
                    qs = icc % 2
                    off = (icc % 4 // 2) * 512
                    e.tensor_copy(h2b[:, ic, :],
                                  h28[:, qs, off:off + 512]).then_inc(DV.h, 1)
                em("dv", [(AC, h2q_ac[(tt, half, icc)]),
                          (DV, vB["dv"] if (tt == 0 and half == 0
                                            and icc == 0) else 0),
                          (PE, dmm_pe.get((tt - 2, HCN - 1), 0)
                           if (half == 0 and icc == 0) else 0)], f)
                h2c_dv[(tt, half, icc)] = DV.bump()
            h2_dv[(tt, half)] = h2c_dv[(tt, half, 10)]

    def emit_qyl(tt):
        qyl[tt] = (QYL, dma("gp", QYL, qyld4[:, :, :],
                            qy_h.rearrange("(g p) n -> p g n", p=128)
                            [:, 0:4, tt * 512:(tt + 1) * 512],
                            qys_totals + [(PE, vB["pe"]), (AC, vB["ac"]),
                            (GP, vB["gp"]),
                            (DV, ds_dv.get((tt - 1, HCN - 1), 0))]))
        qyl2 = dma("gp", QYL, af[:, :, :],
                   qy_h.rearrange("(g p) n -> p g n", p=128)
                   [:, 4:6, tt * 512:(tt + 1) * 512], [])
        qyl3 = dma("gp", QYL, yf[:, :, :],
                   qy_h.rearrange("(g p) n -> p g n", p=128)
                   [:, 6:8, tt * 512:(tt + 1) * 512], [])
        qyl[tt] = (QYL, qyl3)

    def emit_dmm(tt, hc):
        h2b = h2b2[tt % 2]
        if True:
            gs = hc % 2
            bank = pb[gs]

            def f(e, tt=tt, hc=hc, bank=bank, h2b=h2b):
                for ic in range(ICN):
                    ins = e.matmul(bank[:, :], qwd[:, ic, hc, :],
                                   h2b[:, ic, :],
                                   start=ic == 0, stop=ic == ICN - 1)
                ins.then_inc(PE.h, 1)
            em("pe", [(DV, h2_dv[(tt, 1)]),
                      (DV, ds_dv.get((tt, hc - 2), 0) or
                       (ds_dv.get((tt - 1, HCN - 2 + hc), 0) if hc < 2 else 0)),
                      (DV, v_bc2c if (tt == 0 and hc == 0) else 0)], f)
            dmm_pe[(tt, hc)] = PE.bump()

            def f(e, tt=tt, hc=hc, bank=bank):
                e.scalar_tensor_tensor(outsum[:, hc, :], bank[:, :],
                                       sdown[:, hc:hc + 1], qyv[hc],
                                       ALU.mult, ALU.add)
                e.drain().then_inc(DV.h, 1)
            em("dv", [(PE, dmm_pe[(tt, hc)]), qyl[tt],
                      (PE, otr_pe.get((tt - 1, 3), 0) if hc < 2 else 0)], f)
            ds_dv[(tt, hc)] = DV.bump()

    def emit_cout(tt, sub):
        if True:
            i = tt * 4 + sub
            slot = i % 2
            xld3[i] = (XL[slot], dma(
                "gp", XL[slot], xbuf[:, slot, :],
                x_h[i * 128:(i + 1) * 128, :],
                [(AC, vB["ac"]), (DV, oev_dv.get(i - 2, 0))]))

            base = 6 if sub % 2 == 0 else 4

            def f(e, tt=tt, sub=sub, base=base):
                for hc in range(HCN):
                    bank = pb[base] if hc < 4 else pb[base + 1]
                    ins = e.transpose(bank[:, (hc % 4) * 128:(hc % 4 + 1) * 128],
                                      outsum[:, hc, sub * 128:(sub + 1) * 128],
                                      idf[:])
                    if hc in (3, 7):
                        ins.then_inc(PE.h, 1)
            em("pe", [(DV, ds_dv[(tt, HCN - 1)]),
                      (DV, oev_dv.get(tt * 4 + sub - 2, 0)),
                      (PE, v_bc2 if (tt == 0 and sub == 0) else 0),
                      (DV, vB["dv"] if (tt == 0 and sub == 1) else 0)], f)
            otr_pe[(tt, sub)] = PE.bump(2)

            def f(e, i=i, tt=tt, sub=sub, slot=slot, base=base):
                for half in range(2):
                    bank = pb[base + half]
                    e.tensor_tensor(
                        outtm[:, slot, half * 512:(half + 1) * 512],
                        bank[:, 0:512],
                        xbuf[:, slot, half * 512:(half + 1) * 512],
                        ALU.add).then_inc(DV.h, 1)
            em("dv", [(PE, otr_pe[(tt, sub)]), xld3[i],
                      out_st.get(i - 2, (OS[0], 0))], f)
            oev_dv[i] = DV.bump(2)

            out_st[i] = (OS[slot], dma("sp", OS[slot],
                                       out_h[i * 128:(i + 1) * 128, :],
                                       outtm[:, slot, :], [(DV, oev_dv[i])]))

    emit_cquant(0, 0)
    emit_cquant(0, 1)
    for tt in range(NTT):
        emit_qyl(tt)
        for hc in range(4):
            emit_dmm(tt, hc)
        if tt + 1 < NTT:
            emit_cquant(tt + 1, 0)
            emit_cquant(tt + 1, 1)
        for hc in range(4, HCN):
            emit_dmm(tt, hc)
        for sub in range(4):
            emit_cout(tt, sub)

    em("sp", [(OS[0], OS[0].v), (OS[1], OS[1].v)], lambda e: e.nop())

    with nc.Block() as block:
        @block.sync
        def _(e):
            for f_ in ops["sp"]:
                f_(e)

        @block.gpsimd
        def _(e):
            for f_ in ops["gp"]:
                f_(e)

        @block.tensor
        def _(e):
            for f_ in ops["pe"]:
                f_(e)

        @block.vector
        def _(e):
            for f_ in ops["dv"]:
                f_(e)

        @block.scalar
        def _(e):
            for f_ in ops["ac"]:
                f_(e)

    return nc


_NC_CACHE = None


def kernel(L0_attn_x, L0_attn_w_qkva, L0_attn_gamma, L0_ffn_clip_x,
           L0_ffn_clip_down, L0_ffn_w_gate, L0_ffn_w_up, L0_ffn_w_down):
    global _NC_CACHE
    if _NC_CACHE is None:
        _NC_CACHE = build()
    nc = _NC_CACHE

    x = np.ascontiguousarray(np.asarray(L0_attn_x, dtype=np.float32))
    wq = np.ascontiguousarray(np.asarray(L0_attn_w_qkva, dtype=np.float32).T)
    gam = np.ascontiguousarray(
        np.asarray(L0_attn_gamma, dtype=np.float32).reshape(8, 128))
    wg = np.ascontiguousarray(np.asarray(L0_ffn_w_gate, dtype=np.float32))
    wu = np.ascontiguousarray(np.asarray(L0_ffn_w_up, dtype=np.float32))
    wd = np.ascontiguousarray(np.asarray(L0_ffn_w_down, dtype=np.float32))
    clips = np.stack([np.asarray(L0_ffn_clip_x, dtype=np.float32).ravel()[0],
                      np.asarray(L0_ffn_clip_down,
                                 dtype=np.float32).ravel()[0]]).reshape(1, 2)
    idf = np.eye(128, dtype=np.float32)

    in_maps = []
    for c in range(8):
        in_maps.append({
            "x": np.ascontiguousarray(x[c]),
            "w_qkva": wq, "gamma": gam, "w_gate": wg, "w_up": wu,
            "w_down": wd, "clips": clips, "ident_f": idf,
        })
    res = run_bass_kernel_spmd(nc, in_maps, core_ids=list(range(8)))
    out = np.stack([res.results[c]["out"] for c in range(8)], axis=0)
    return out
